# version 04
# baseline (speedup 1.0000x reference)
import numpy as np
from contextlib import ExitStack

import concourse.bass as bass
import concourse.mybir as mybir
from concourse.bass_utils import run_bass_kernel_spmd

F32 = mybir.dt.float32
BF16 = mybir.dt.bfloat16
I8 = mybir.dt.int8
AF = mybir.ActivationFunctionType
ALU = mybir.AluOpType

TOK = 2048
D = 2048
H = 16
DH = 128
BLK = 256
NBLK = TOK // BLK
FFN = 8192
NTC = TOK // 128
NFC = D // 128


class Ctr:

    def __init__(self, stack, nc, name):
        self.sem = stack.enter_context(nc.semaphore(name))
        self.n = 0

    def inc(self, ins, k=1):
        ins.then_inc(self.sem, k)
        self.n += k
        return self.n

    def wait(self, eng, at=None):
        eng.wait_ge(self.sem, self.n if at is None else at)


class Emit:
    def __init__(self, nc, stack, block):
        self.nc = nc
        self.stack = stack
        self.block = block
        self.ctr = {}
        self.t = {}

    def c(self, name):
        if name not in self.ctr:
            self.ctr[name] = Ctr(self.stack, self.nc, name)
        return self.ctr[name]



def emit_setup(E, seg):
    nc = E.nc
    st = E.stack.enter_context
    E.t["ident"] = st(nc.sbuf_tensor("ident", [128, 128], BF16))
    E.t["slope_c"] = st(nc.sbuf_tensor("sb_slope_c", [128, 16], F32))
    E.t["negslope"] = st(nc.sbuf_tensor("negslope", [128, 16], F32))
    E.t["bd_cols"] = st(nc.sbuf_tensor("bd_cols", [128, 16], F32))
    E.t["selp"] = st(nc.sbuf_tensor("sb_selp", [128, 1], F32))
    E.t["fence_s"] = st(nc.sbuf_tensor("fence_s", [128, 128], F32))
    E.t["cc"] = st(nc.sbuf_tensor("cc", [128, 4], F32))
    if seg in ("B", "FULL"):
        E.t["qdecg_iota"] = st(nc.sbuf_tensor("qdecg_iota", [128, 16], F32))
        E.t["qdecg_t"] = st(nc.sbuf_tensor("qdecg_t", [128, H * 16], F32))

    ld = E.c("ld_const")
    gp = E.c("gp_setup")
    vs = E.c("v_setup")
    E.setup_done = E.c("s_setup")

    @E.block.sync
    def _(sync):
        ld.inc(sync.dma_start(out=E.t["slope_c"][:], in_=E.d["slope_c"]), 16)
        ld.inc(sync.dma_start(out=E.t["selp"][:], in_=E.d["selp"]), 16)

    @E.block.gpsimd
    def _(g):
        g.memset(E.t["ident"][:], 0)
        g.affine_select(out=E.t["ident"][:], in_=E.t["ident"][:],
                        compare_op=ALU.not_equal, fill=1.0, base=0,
                        pattern=[[-1, 128]], channel_multiplier=1)
        g.memset(E.t["cc"][:, 0:1], 1e-5)
        g.memset(E.t["cc"][:, 1:2], 1e-6)
        g.memset(E.t["cc"][:, 2:3], 1.0)
        ins = g.memset(E.t["cc"][:, 3:4], 0.0)
        if seg in ("B", "FULL"):
            ins = g.iota(E.t["qdecg_iota"][:], [[128, 16]], base=1, channel_multiplier=1,
                         allow_small_or_imprecise_dtypes=True)
        gp.inc(ins)

    @E.block.vector
    def _(v):
        ld.wait(v)
        E.c("v_setup").inc(v.tensor_scalar_mul(
            out=E.t["negslope"][:], in0=E.t["slope_c"][:], scalar1=-1.0))

    @E.block.scalar
    def _(s):
        ld.wait(s)
        gp.wait(s)
        vs.wait(s)
        s.activation(out=E.t["bd_cols"][:], in_=E.t["slope_c"][:], func=AF.Exp,
                     scale=-float(BLK))
        for h in range(H):
            nsl = E.t["negslope"][:, h:h + 1]
            if seg in ("B", "FULL"):
                for tc in range(16):
                    s.activation(out=E.t["qdecg_t"][:, h * 16 + tc:h * 16 + tc + 1],
                                 in_=E.t["qdecg_iota"][:, tc:tc + 1], func=AF.Exp, scale=nsl)
        E.setup_done.inc(s.copy(out=E.t["fence_s"][:, 0:1], in_=E.t["bd_cols"][:, 0:1]))


def emit_attn_consts(E, e):
    nc = E.nc
    E.t["idx0"] = e(nc.sbuf_tensor("idx0", [128, 256], F32))
    E.t["idx1"] = e(nc.sbuf_tensor("idx1", [128, 256], F32))
    E.t["qdec_iota"] = e(nc.sbuf_tensor("qdec_iota", [128, 2], F32))
    E.t["kdec_iota"] = e(nc.sbuf_tensor("kdec_iota", [128, 2], F32))
    E.t["Dtab"] = e(nc.sbuf_tensor("Dtab", [128, H * 2 * 256], BF16))
    E.t["qdec_t"] = e(nc.sbuf_tensor("qdec_t", [128, H * 2], F32))
    E.t["kdec_t"] = e(nc.sbuf_tensor("kdec_t", [128, H * 2], F32))
    gp2 = E.c("gp_aconst")
    sc2 = E.c("sc_aconst")

    @E.block.gpsimd
    def _(g):
        for kc, t in ((0, "idx0"), (1, "idx1")):
            g.iota(E.t[t][:], [[1, 256]], base=-128 * kc, channel_multiplier=-1,
                   allow_small_or_imprecise_dtypes=True)
            g.affine_select(out=E.t[t][:], in_=E.t[t][:],
                            compare_op=ALU.is_ge, fill=1e30, base=-128 * kc,
                            pattern=[[1, 256]], channel_multiplier=-1)
        g.iota(E.t["qdec_iota"][:], [[128, 2]], base=1, channel_multiplier=1,
               allow_small_or_imprecise_dtypes=True)
        gp2.inc(g.iota(E.t["kdec_iota"][:], [[-128, 2]], base=255, channel_multiplier=-1,
                       allow_small_or_imprecise_dtypes=True))

    @E.block.scalar
    def _(s):
        gp2.wait(s)
        E.c("v_setup").wait(s)
        Dt = E.t["Dtab"]
        ins = None
        for h in range(H):
            nsl = E.t["negslope"][:, h:h + 1]
            for kc in range(2):
                s.activation(out=Dt[:, (h * 2 + kc) * 256:(h * 2 + kc + 1) * 256],
                             in_=E.t["idx" + str(kc)][:], func=AF.Exp, scale=nsl)
                s.activation(out=E.t["qdec_t"][:, h * 2 + kc:h * 2 + kc + 1],
                             in_=E.t["qdec_iota"][:, kc:kc + 1], func=AF.Exp, scale=nsl)
                ins = s.activation(out=E.t["kdec_t"][:, h * 2 + kc:h * 2 + kc + 1],
                                   in_=E.t["kdec_iota"][:, kc:kc + 1], func=AF.Exp, scale=nsl)
        sc2.inc(ins)
    E.aconst_done = (sc2, sc2.n)


def scalar_fence(E, wait_list, ph):
    @E.block.scalar
    def _(s):
        for item in wait_list:
            if item is None:
                continue
            ctr, val = item
            ctr.wait(s, val)
        ph.inc(s.copy(out=E.t["fence_s"][:, 0:1], in_=E.t["fence_s"][:, 0:1]))



def emit_ln(E, src_dram, tag):
    nc = E.nc
    st = E.stack.enter_context
    if "xnf" not in E.t:
        E.t["xnf"] = st(nc.sbuf_tensor("xnf", [128, NFC * TOK], BF16))
    xnf = E.t["xnf"]
    phn = E.c("ph").n
    with ExitStack() as phs:
        e = phs.enter_context
        xtile = [e(nc.sbuf_tensor(f"ln_x{i}_{tag}", [128, D], F32)) for i in range(2)]
        xc = e(nc.sbuf_tensor(f"ln_xc_{tag}", [128, D], F32))
        sq = e(nc.sbuf_tensor(f"ln_sq_{tag}", [128, D], F32))
        xntm = [e(nc.sbuf_tensor(f"ln_xn{i}_{tag}", [128, D], BF16)) for i in range(2)]
        mcol = e(nc.sbuf_tensor(f"ln_m_{tag}", [128, 1], F32))
        m2 = e(nc.sbuf_tensor(f"ln_m2_{tag}", [128, 1], F32))
        ssq = e(nc.sbuf_tensor(f"ln_ssq_{tag}", [128, 1], F32))
        sd = e(nc.sbuf_tensor(f"ln_sd_{tag}", [128, 1], F32))
        rstd = e(nc.sbuf_tensor(f"ln_rstd_{tag}", [128, 1], F32))
        pst = [e(nc.psum_tensor(f"ln_ps{i}_{tag}", [128, 128], BF16)) for i in range(4)]

        ld = E.c("ld_ln")
        va = E.c("va_ln")
        vb = E.c("vb_ln")
        vf = E.c("vf_ln")
        sa = E.c("sa_ln")
        vc = E.c("vc_ln")
        tp = E.c("tp_ln")
        se = E.c("se_ln")
        ld0, vb0, vf0, sa0, vc0, tp0, se0 = ld.n, vb.n, vf.n, sa.n, vc.n, tp.n, se.n

        @E.block.sync
        def _(sync):
            for tc in range(NTC):
                if tc >= 2:
                    vf.wait(sync, vf0 + tc - 1)
                ld.inc(sync.dma_start(
                    out=xtile[tc % 2][:], in_=src_dram[tc * 128:(tc + 1) * 128, :]), 16)

        @E.block.vector
        def _(v):
            E.c("ph").wait(v, phn)
            for tc in range(NTC):
                ld.wait(v, ld0 + 16 * (tc + 1))
                va.inc(v.reduce_sum(mcol[:], xtile[tc % 2][:], axis=mybir.AxisListType.X))
                va.wait(v)
                va.inc(v.tensor_scalar_mul(out=m2[:], in0=mcol[:], scalar1=1.0 / D))
                va.wait(v)
                vb.inc(v.tensor_scalar(out=xc[:], in0=xtile[tc % 2][:],
                                       scalar1=m2[:], scalar2=None, op0=ALU.subtract))
                vf.inc(v.tensor_copy(out=mcol[:], in_=m2[:]))
                sa.wait(v, sa0 + 2 * (tc + 1))
                va.inc(v.reciprocal(rstd[:], sd[:]))
                va.wait(v)
                if tc >= 2:
                    tp.wait(v, tp0 + (tc - 1) * NFC)
                vc.inc(v.tensor_scalar_mul(out=xntm[tc % 2][:], in0=xc[:], scalar1=rstd[:]))

        @E.block.scalar
        def _(s):
            E.c("ph").wait(s, phn)

            def evac_tile(tc):
                for fc in range(NFC):
                    k = tc * NFC + fc
                    tp.wait(s, tp0 + k + 1)
                    se.inc(s.copy(out=xnf[:, fc * TOK + tc * 128: fc * TOK + (tc + 1) * 128],
                                  in_=pst[k % 4][:]))

            for tc in range(NTC):
                vb.wait(s, vb0 + tc + 1)
                sa.inc(s.activation(out=sq[:], in_=xc[:], func=AF.Square, accum_out=ssq[:]))
                sa.inc(s.activation(out=sd[:], in_=ssq[:], func=AF.Sqrt,
                                    scale=1.0 / D, bias=E.t["cc"][:, 0:1]))
                if tc >= 1:
                    evac_tile(tc - 1)
            evac_tile(NTC - 1)

        @E.block.tensor
        def _(t):
            E.c("ph").wait(t, phn)
            for tc in range(NTC):
                vc.wait(t, vc0 + tc + 1)
                for fc in range(NFC):
                    k = tc * NFC + fc
                    if k >= 4:
                        se.wait(t, se0 + k - 3)
                    tp.inc(t.transpose(pst[k % 4][:], xntm[tc % 2][:, fc * 128:(fc + 1) * 128],
                                       E.t["ident"][:]))

        E.ln_done = (se, se.n)



def emit_attn_local(E, depth="blocks"):
    nc = E.nc
    phn = E.c("ph").n
    with ExitStack() as phs:
        e = phs.enter_context
        wq = [e(nc.sbuf_tensor(f"wq{i}", [128, NFC * 128], BF16)) for i in range(2)]
        wk = [e(nc.sbuf_tensor(f"wk{i}", [128, NFC * 128], BF16)) for i in range(2)]
        wv = [e(nc.sbuf_tensor(f"wv{i}", [128, NFC * 128], BF16)) for i in range(2)]
        qf = [e(nc.sbuf_tensor(f"qf{i}", [128, TOK], BF16)) for i in range(3)]
        kf = [e(nc.sbuf_tensor(f"kf{i}", [128, TOK], BF16)) for i in range(3)]
        vf_ = [e(nc.sbuf_tensor(f"vf{i}", [128, TOK], BF16)) for i in range(3)]
        ktd = [e(nc.sbuf_tensor(f"ktd{i}", [128, TOK], BF16)) for i in range(2)]
        vtm = [e(nc.sbuf_tensor(f"vtm{i}", [128, TOK], BF16)) for i in range(2)]
        qkTd = e(nc.sbuf_tensor("qkTd", [128, 512], BF16))
        t1 = [e(nc.sbuf_tensor(f"t1_{i}", [128, 128], F32)) for i in range(2)]
        outsb = [e(nc.sbuf_tensor(f"attn_o{i}", [128, 128], F32)) for i in range(4)]
        kv = e(nc.sbuf_tensor("kv", [128, 128], F32))
        kvbf = e(nc.sbuf_tensor("kvbf", [128, 128], BF16))
        psA = e(nc.psum_tensor("psA", [128, 512], F32))
        psB = e(nc.psum_tensor("psB", [128, 512], F32))
        psT = [e(nc.psum_tensor(f"psT{i}", [128, 1024], BF16)) for i in range(2)]
        psQK = e(nc.psum_tensor("psQK", [128, 512], F32))
        psIE = [e(nc.psum_tensor(f"psIE{i}", [128, 512], F32)) for i in range(2)]
        psKV = e(nc.psum_tensor("psKV", [128, 512], F32))

        gq = E.c("gq_attn")
        mmq = E.c("mmq")
        evq = E.c("evq")
        tpa = E.c("tpa")
        eva = E.c("eva")
        kvb = E.c("kvb")
        qkm = E.c("qkm")
        qkd = E.c("qkd")
        iem = E.c("iem")
        t1r = E.c("t1r")
        oadd = E.c("oadd")
        kvm = E.c("kvm")
        kvu = E.c("kvu")
        msk = E.c("msk")
        spl = E.c("spl")
        (gq0, mmq0, evq0, tpa0, eva0, kvb0, qkm0, qkd0, iem0, t1r0, oadd0,
         kvm0, kvu0, msk0, spl0) = (gq.n, mmq.n, evq.n, tpa.n, eva.n, kvb.n,
                                    qkm.n, qkd.n, iem.n, t1r.n, oadd.n,
                                    kvm.n, kvu.n, msk.n, spl.n)
        KI = {"q": 0, "k": 1, "v": 2}

        def v_mmq(h, ki, tg):
            return mmq0 + h * 12 + ki * 4 + tg + 1

        def v_evq(h, ki, tg):
            return evq0 + h * 12 + ki * 4 + tg + 1

        def v_tpa(h, idx, tc):
            return tpa0 + h * 32 + idx * 16 + tc + 1

        def v_eva(h, idx, tc):
            return eva0 + h * 32 + idx * 16 + tc + 1

        per_head_spl = 33 if depth == "blocks" else 16

        def v_spl_q(h, tc):
            return spl0 + 16 * (h * per_head_spl + tc + 1)

        def v_spl_attn(h, l):
            return spl0 + 16 * (h * per_head_spl + 16 + l + 1)

        def v_spl_S(h):
            return spl0 + 16 * (h * per_head_spl + 33)

        @E.block.gpsimd
        def _(g):
            E.c("ph").wait(g, phn)
            for h in range(H):
                hb = h % 2
                if h >= 2:
                    mmq.wait(g, v_mmq(h - 2, 2, 3))
                for w_sb, col0 in ((wq, h * 128), (wk, D + h * 128), (wv, 2 * D + h * 128)):
                    gq.inc(g.dma_start(
                        out=w_sb[hb][:].rearrange("p (c o) -> p c o", o=128),
                        in_=E.d["wqkv_t"][:, col0:col0 + 128]
                        .rearrange("(c p) o -> p c o", p=128)), 16)

        @E.block.tensor
        def _(t):
            E.c("ph").wait(t, phn)

            def qkv_groups(h):
                hb = h % 2
                groups = []

                def first_grp():
                    gq.wait(t, gq0 + (h + 1) * 48)
                groups.append(first_grp)
                for kind, w_sb, dst_ps in (("q", wq, psA), ("k", wk, psB), ("v", wv, psA)):
                    ki = KI[kind]
                    for tg in range(4):
                        def mmgrp(kind=kind, ki=ki, tg=tg, w_sb=w_sb, dst_ps=dst_ps):
                            if tg > 0:
                                prev = v_evq(h, ki, tg - 1)
                            elif kind == "q":
                                prev = v_evq(h - 1, 2, 3) if h > 0 else None
                            elif kind == "k":
                                prev = v_evq(h - 1, 1, 3) if h > 0 else None
                            else:
                                prev = v_evq(h, 0, 3)
                            if prev is not None:
                                evq.wait(t, prev)
                            for fc in range(NFC):
                                ins = t.matmul(
                                    dst_ps[:], w_sb[hb][:, fc * 128:(fc + 1) * 128],
                                    E.t["xnf"][:, fc * TOK + tg * 512: fc * TOK + (tg + 1) * 512],
                                    start=(fc == 0), stop=(fc == NFC - 1))
                            mmq.inc(ins)
                        groups.append(mmgrp)
                for idx, src_sb in ((0, kf), (1, vf_)):
                    ki = 1 + idx
                    for tcg in range(2):
                        def trgrp(idx=idx, ki=ki, tcg=tcg, src_sb=src_sb):
                            for tc in range(tcg * 8, tcg * 8 + 8):
                                k = idx * 16 + tc
                                evq.wait(t, v_evq(h, ki, tc // 4))
                                gk = h * 32 + k
                                if gk >= 2:
                                    eva.wait(t, eva0 + gk - 1)
                                tpa.inc(t.transpose(psT[k % 2][:, 0:128],
                                                    src_sb[h % 3][:, tc * 128:(tc + 1) * 128],
                                                    E.t["ident"][:]))
                        groups.append(trgrp)
                return groups

            def block_groups(h):
                hb = h % 2
                groups = []

                def head_grp():
                    evq.wait(t, v_evq(h, 2, 3))
                    eva.wait(t, v_eva(h, 1, 15))
                groups.append(head_grp)
                for i in range(NBLK):
                    def blk(i=i):
                        kvb.wait(t, kvb0 + h * 8 + i + 1)
                        if i > 0 or h > 0:
                            qkd.wait(t, qkd0 + h * 8 + i)
                        for kc in range(2):
                            ins = t.matmul(psQK[:, kc * 256:(kc + 1) * 256],
                                           kf[h % 3][:, (2 * i + kc) * 128:(2 * i + kc + 1) * 128],
                                           qf[h % 3][:, i * 256:(i + 1) * 256],
                                           start=True, stop=True)
                        qkm.inc(ins)
                        qkd.wait(t, qkd0 + h * 8 + i + 1)
                        for mc in range(2):
                            s_glob = h * 16 + i * 2 + mc
                            pse = psIE[s_glob % 2]
                            if s_glob >= 2:
                                oadd.wait(t, oadd0 + s_glob - 1)
                            for kc in range(2):
                                t.matmul(pse[:, 0:128],
                                         qkTd[:, kc * 256 + mc * 128: kc * 256 + (mc + 1) * 128],
                                         vtm[hb][:, (2 * i + kc) * 128:(2 * i + kc + 1) * 128],
                                         start=(kc == 0), stop=(kc == 1))
                            ins = t.matmul(pse[:, 128:256],
                                           qf[h % 3][:, (i * 2 + mc) * 128:(i * 2 + mc + 1) * 128],
                                           kvbf[:], start=True, stop=True)
                            iem.inc(ins)
                        if i > 0 or h > 0:
                            kvu.wait(t, kvu0 + h * 8 + i)
                        for kc in range(2):
                            ins = t.matmul(psKV[:, 0:128],
                                           ktd[hb][:, (2 * i + kc) * 128:(2 * i + kc + 1) * 128],
                                           vtm[hb][:, (2 * i + kc) * 128:(2 * i + kc + 1) * 128],
                                           start=(kc == 0), stop=(kc == 1))
                        kvm.inc(ins)
                    groups.append(blk)
                return groups

            def interleave(g, b):
                bi = 0
                for j, grp in enumerate(g):
                    grp()
                    while bi < len(b) and bi * len(g) <= (j + 1) * len(b):
                        b[bi]()
                        bi += 1
                while bi < len(b):
                    b[bi]()
                    bi += 1

            for grp in qkv_groups(0) + qkv_groups(1):
                grp()
            for h in range(H - 2):
                g = qkv_groups(h + 2)
                interleave(g[:13], block_groups(h))
                for grp in g[13:]:
                    grp()
            for grp in block_groups(H - 2) + block_groups(H - 1):
                grp()

        @E.block.scalar
        def _(s):
            E.c("ph").wait(s, phn)

            def evac_groups(h):
                groups = []
                for kind, dst_ps, dst_sb in (("q", psA, qf), ("k", psB, kf), ("v", psA, vf_)):
                    ki = KI[kind]
                    for tg in range(4):
                        def ev(kind=kind, ki=ki, tg=tg, dst_ps=dst_ps, dst_sb=dst_sb):
                            if kind == "q" and tg == 0 and h >= 3:
                                spl.wait(s, v_spl_q(h - 3, 15))
                            mmq.wait(s, v_mmq(h, ki, tg))
                            evq.inc(s.activation(out=dst_sb[h % 3][:, tg * 512:(tg + 1) * 512],
                                                 in_=dst_ps[:], func=AF.Silu))
                        groups.append(ev)
                return groups

            def t1_groups(h):
                groups = []
                for i in range(NBLK if depth == "blocks" else 0):
                    for mc in range(2):
                        def t1g(i=i, mc=mc):
                            sl = h * 16 + i * 2 + mc
                            iem.wait(s, iem0 + sl + 1)
                            if sl >= 2:
                                oadd.wait(s, oadd0 + sl - 1)
                            t1r.inc(s.activation(
                                out=t1[sl % 2][:], in_=psIE[sl % 2][:, 128:256],
                                func=AF.Copy,
                                scale=E.t["qdec_t"][:, h * 2 + mc:h * 2 + mc + 1]))
                        groups.append(t1g)
                return groups

            def interleave(g, b):
                bi = 0
                for j, grp in enumerate(g):
                    grp()
                    while bi < len(b) and bi * len(g) <= (j + 1) * len(b):
                        b[bi]()
                        bi += 1
                while bi < len(b):
                    b[bi]()
                    bi += 1

            for grp in evac_groups(0) + evac_groups(1):
                grp()
            for h in range(H - 2):
                interleave(evac_groups(h + 2), t1_groups(h))
            for grp in t1_groups(H - 2) + t1_groups(H - 1):
                grp()

        @E.block.vector
        def _(v):
            E.c("ph").wait(v, phn)

            def eva_groups(h):
                hb = h % 2
                groups = []
                for idx in range(2):
                    for tcg in range(2):
                        def evg(idx=idx, tcg=tcg):
                            for tc in range(tcg * 8, tcg * 8 + 8):
                                k = idx * 16 + tc
                                tpa.wait(v, v_tpa(h, idx, tc))
                                if idx == 0:
                                    cp = tc % 2
                                    ins = v.tensor_scalar_mul(
                                        out=ktd[hb][:, tc * 128:(tc + 1) * 128],
                                        in0=psT[k % 2][:, 0:128],
                                        scalar1=E.t["kdec_t"][:, h * 2 + cp:h * 2 + cp + 1])
                                else:
                                    ins = v.tensor_copy(out=vtm[hb][:, tc * 128:(tc + 1) * 128],
                                                        in_=psT[k % 2][:, 0:128])
                                eva.inc(ins)
                        groups.append(evg)
                return groups

            def vblock_groups(h):
                hb = h % 2
                groups = []

                def mskg():
                    if h >= 1:
                        spl.wait(v, v_spl_S(h - 1))
                    msk.inc(v.memset(kv[:], 0.0))
                groups.append(mskg)
                for i in range(NBLK):
                    def vblk(i=i):
                        if i == 0:
                            msk.wait(v, msk0 + h + 1)
                        kvb.inc(v.tensor_copy(out=kvbf[:], in_=kv[:]))
                        qkm.wait(v, qkm0 + h * 8 + i + 1)
                        Dt = E.t["Dtab"]
                        for kc in range(2):
                            ins = v.tensor_tensor(
                                out=qkTd[:, kc * 256:(kc + 1) * 256],
                                in0=psQK[:, kc * 256:(kc + 1) * 256],
                                in1=Dt[:, (h * 2 + kc) * 256:(h * 2 + kc + 1) * 256],
                                op=ALU.mult)
                        qkd.inc(ins)
                        for mc in range(2):
                            sl = h * 16 + i * 2 + mc
                            t1r.wait(v, t1r0 + sl + 1)
                            if sl >= 4:
                                h2, l2 = (sl - 4) // 16, (sl - 4) % 16
                                spl.wait(v, v_spl_attn(h2, l2))
                            oadd.inc(v.tensor_tensor(out=outsb[sl % 4][:], in0=t1[sl % 2][:],
                                                     in1=psIE[sl % 2][:, 0:128],
                                                     op=ALU.add))
                        kvm.wait(v, kvm0 + h * 8 + i + 1)
                        kvu.inc(v.scalar_tensor_tensor(out=kv[:], in0=kv[:],
                                                       scalar=E.t["bd_cols"][:, h:h + 1],
                                                       in1=psKV[:, 0:128],
                                                       op0=ALU.mult, op1=ALU.add))
                    groups.append(vblk)
                return groups

            def interleave(g, b):
                bi = 0
                for j, grp in enumerate(g):
                    grp()
                    while bi < len(b) and bi * len(g) <= (j + 1) * len(b):
                        b[bi]()
                        bi += 1
                while bi < len(b):
                    b[bi]()
                    bi += 1

            for grp in eva_groups(0) + eva_groups(1):
                grp()
            for h in range(H - 2):
                for grp in vblock_groups(h) + eva_groups(h + 2):
                    grp()
            for grp in vblock_groups(H - 2) + vblock_groups(H - 1):
                grp()

        @E.block.sync
        def _(sync):
            for h in range(H):
                hb = h % 2
                evq.wait(sync, v_evq(h, 0, 3))
                for tc in range(NTC):
                    spl.inc(sync.dma_start(
                        out=E.d["q_s"][h, tc], in_=qf[h % 3][:, tc * 128:(tc + 1) * 128]), 16)
                for i in range(NBLK if depth == "blocks" else 0):
                    for mc in range(2):
                        sl = h * 16 + i * 2 + mc
                        oadd.wait(sync, oadd0 + sl + 1)
                        r0 = i * 256 + mc * 128
                        spl.inc(sync.dma_start(
                            out=E.d["attn_s"][r0:r0 + 128, h * 128:(h + 1) * 128],
                            in_=outsb[sl % 4][:]), 16)
                if depth == "blocks":
                    kvu.wait(sync, kvu0 + h * 8 + 8)
                    spl.inc(sync.dma_start(
                        out=E.d["S_out"][h * 128:(h + 1) * 128, :], in_=kv[:]), 16)

        E.attn_done = (spl, spl.n)



def _core_inputs_A(inputs, core):
    b, half = core // 2, core % 2
    x = np.ascontiguousarray(
        np.asarray(inputs["L0_attn_hidden_states"])[b, half * TOK:(half + 1) * TOK])
    wqkv_t = np.ascontiguousarray(np.asarray(inputs["L0_attn_w_qkv"]).T)
    slope_c = np.ascontiguousarray(
        np.broadcast_to(np.asarray(inputs["L0_attn_slope_rate"])[None, :], (128, H)))
    selp = np.full((128, 1), 1.0 if half == 1 else 0.0, np.float32)
    return dict(x=x, wqkv_t=wqkv_t, slope_c=slope_c, selp=selp)


def build_A(upto="attn"):
    nc = bass.Bass("TRN2", target_bir_lowering=False, debug=False, detect_race_conditions=False)
    d = {}
    d["x"] = nc.dram_tensor("x", [TOK, D], F32, kind="ExternalInput").ap()
    d["wqkv_t"] = nc.dram_tensor("wqkv_t", [D, 3 * D], F32, kind="ExternalInput").ap()
    d["slope_c"] = nc.dram_tensor("slope_c", [128, H], F32, kind="ExternalInput").ap()
    d["selp"] = nc.dram_tensor("selp", [128, 1], F32, kind="ExternalInput").ap()
    d["attn_s"] = nc.dram_tensor("attn_s", [TOK, D], F32, kind="ExternalOutput").ap()
    d["q_s"] = nc.dram_tensor("q_s", [H, NTC, 128, 128], BF16, kind="ExternalOutput").ap()
    d["S_out"] = nc.dram_tensor("S_out", [D, DH], F32, kind="ExternalOutput").ap()
    with ExitStack() as stack:
        block = stack.enter_context(nc.Block())
        E = Emit(nc, stack, block)
        E.d = d
        ph = E.c("ph")
        emit_setup(E, "A")
        emit_attn_consts(E, stack.enter_context)
        scalar_fence(E, [(E.setup_done, E.setup_done.n), E.aconst_done], ph)
        if upto != "setup":
            emit_ln(E, d["x"], "l0")
            scalar_fence(E, [E.ln_done], ph)
        if upto.startswith("attn"):
            emit_attn_local(E, depth=upto[5:] or "blocks")
        if upto != "attn":
            dn = E.c("drain")

            @E.block.sync
            def _(sync):
                E.c("ph").wait(sync)
                dn.inc(sync.dma_start(out=d["S_out"][0:1, 0:128], in_=E.t["fence_s"][0:1, :]), 16)

            @E.block.gpsimd
            def _(g):
                if upto.startswith("attn"):
                    ctr, n = E.attn_done
                    ctr.wait(g, n)
                dn.wait(g)
        else:
            @E.block.gpsimd
            def _(g):
                ctr, n = E.attn_done
                ctr.wait(g, n)
    return nc



def emit_setup2(E, seg):
    nc = E.nc
    st = E.stack.enter_context
    ld = E.c("ld_const2")
    need_b = seg in ("B", "FULL")
    need_c = seg in ("C", "FULL")

    if need_b:
        E.t["normw_b"] = st(nc.sbuf_tensor("sb_normw_b", [128, D], F32))
        E.t["igw"] = st(nc.sbuf_tensor("sb_igw", [128, H * 128], BF16))
        E.t["agw"] = st(nc.sbuf_tensor("sb_agw", [128, H * 128], BF16))
        E.t["igb"] = st(nc.sbuf_tensor("sb_igb", [128, H], F32))
        E.t["agb"] = st(nc.sbuf_tensor("sb_agb", [128, H], F32))
        E.t["apar"] = st(nc.sbuf_tensor("sb_apar", [128, H], F32))
        E.t["m8spa"] = st(nc.sbuf_tensor("sb_m8spa", [128, H], F32))
        E.t["rcol"] = st(nc.sbuf_tensor("sb_rcol", [128, 1], F32))
        E.t["omrcol"] = st(nc.sbuf_tensor("sb_omrcol", [128, 1], F32))
        E.t["spa_t1"] = st(nc.sbuf_tensor("sb_spa_t1", [128, H], F32))
        E.t["spa_t2"] = st(nc.sbuf_tensor("sb_spa_t2", [128, H], F32))
    if need_c:
        E.t["s1g"] = st(nc.sbuf_tensor("sb_s1g", [128, 64], F32))
        E.t["s1u"] = st(nc.sbuf_tensor("sb_s1u", [128, 64], F32))

    @E.block.sync
    def _(sync):
        if need_b:
            ld.inc(sync.dma_start(out=E.t["normw_b"][:], in_=E.d["normw_b"]), 16)
            ld.inc(sync.dma_start(out=E.t["igb"][:], in_=E.d["igb_c"]), 16)
            ld.inc(sync.dma_start(out=E.t["agb"][:], in_=E.d["agb_c"]), 16)
            ld.inc(sync.dma_start(out=E.t["apar"][:], in_=E.d["apar_c"]), 16)
            ld.inc(sync.dma_start(out=E.t["rcol"][:], in_=E.d["rcol"]), 16)
            ld.inc(sync.dma_start(out=E.t["omrcol"][:], in_=E.d["omrcol"]), 16)
        if need_c:
            ld.inc(sync.dma_start(out=E.t["s1g"][:], in_=E.d["s1g_c"]), 16)
            ld.inc(sync.dma_start(out=E.t["s1u"][:], in_=E.d["s1u_c"]), 16)

    if need_b:
        ldg = E.c("ld_const2g")

        @E.block.gpsimd
        def _(g):
            ldg.inc(g.dma_start(out=E.t["igw"][:].rearrange("i (h j) -> i h j", j=128),
                                in_=E.d["igw"].rearrange("h i j -> i h j")), 16)
            ldg.inc(g.dma_start(out=E.t["agw"][:].rearrange("i (h j) -> i h j", j=128),
                                in_=E.d["agw"].rearrange("h i j -> i h j")), 16)

        sp = E.c("sp_setup")

        @E.block.scalar
        def _(s):
            ld.wait(s)
            E.c("gp_setup").wait(s)
            s.activation(out=E.t["spa_t2"][:], in_=E.t["apar"][:], func=AF.Abs)
            s.activation(out=E.t["spa_t2"][:], in_=E.t["spa_t2"][:], func=AF.Exp, scale=-1.0)
            sp.inc(s.activation(out=E.t["spa_t2"][:], in_=E.t["spa_t2"][:], func=AF.Ln,
                                bias=E.t["cc"][:, 2:3]))

        @E.block.vector
        def _(v):
            ld.wait(v)
            sp.inc(v.tensor_scalar_max(out=E.t["spa_t1"][:], in0=E.t["apar"][:],
                                       scalar1=0.0))
            sp.wait(v, 2)
            sp.inc(v.tensor_tensor(out=E.t["m8spa"][:], in0=E.t["spa_t1"][:],
                                   in1=E.t["spa_t2"][:], op=ALU.add))
            sp.wait(v, 3)
            sp.inc(v.tensor_scalar_mul(out=E.t["m8spa"][:], in0=E.t["m8spa"][:],
                                       scalar1=-8.0))
        E.setup2_done = (sp, sp.n)
        E.setup2g_done = (ldg, ldg.n)
    else:
        E.setup2_done = (ld, ld.n)
        E.setup2g_done = None



def emit_sroute(E, src_ap, cc_wait=None, alloc=None):
    nc = E.nc
    st = alloc if alloc is not None else E.stack.enter_context
    E.t["Sbf"] = st(nc.sbuf_tensor("Sbf", [128, H * 128], BF16))
    phn = E.c("ph").n
    ld = E.c("ld_sroute")
    srt = E.c("srt")
    with ExitStack() as phs:
        Ssb = phs.enter_context(nc.sbuf_tensor("Ssb", [128, H * 128], F32))

        @E.block.sync
        def _(sync):
            E.c("ph").wait(sync, phn)
            if cc_wait is not None:
                cc_wait[0].wait(sync, cc_wait[1])
            ld.inc(sync.dma_start(
                out=Ssb[:].rearrange("p (h e) -> p h e", e=128),
                in_=src_ap.rearrange("(h p) e -> p h e", p=128)), 16)

        @E.block.vector
        def _(v):
            E.c("ph").wait(v, phn)
            ld.wait(v)
            v.tensor_scalar_mul(out=Ssb[:], in0=Ssb[:], scalar1=E.t["selp"][:])
            srt.inc(v.tensor_copy(out=E.t["Sbf"][:], in_=Ssb[:]))
    E.sroute_done = (srt, srt.n)



def emit_corr_rms_gate_z(E):
    nc = E.nc
    phn = E.c("ph").n
    with ExitStack() as phs:
        e = phs.enter_context
        wgt = e(nc.sbuf_tensor("wgt", [128, NFC * D], BF16))
        atile = [e(nc.sbuf_tensor(f"cr_at{i}", [128, D], F32)) for i in range(2)]
        qtl = e(nc.sbuf_tensor("cr_q", [128, H * 128], BF16))
        corr = e(nc.sbuf_tensor("cr_c", [128, D], F32))
        af = e(nc.sbuf_tensor("cr_af", [128, D], F32))
        ztm = [e(nc.sbuf_tensor(f"cr_z{i}", [128, D], BF16)) for i in range(2)]
        sg = [e(nc.sbuf_tensor(f"cr_sg{i}", [128, 512], F32)) for i in range(2)]
        zev = [e(nc.sbuf_tensor(f"cr_ze{i}", [128, 128], BF16)) for i in range(4)]
        ssq = e(nc.sbuf_tensor("cr_ssq", [128, 1], F32))
        sd = e(nc.sbuf_tensor("cr_sd", [128, 1], F32))
        rstd = e(nc.sbuf_tensor("cr_rstd", [128, 1], F32))
        psR = [e(nc.psum_tensor(f"cr_psR{i}", [128, 512], F32)) for i in range(2)]
        psG = [e(nc.psum_tensor(f"cr_psG{i}", [128, 512], F32)) for i in range(2)]
        psT = [e(nc.psum_tensor(f"cr_psT{i}", [128, 1024], BF16)) for i in range(2)]

        wld = E.c("cr_wld")
        ald = E.c("cr_ald")
        qld = E.c("cr_qld")
        crm = E.c("cr_crm")
        cre = E.c("cr_cre")
        vaf = E.c("cr_vaf")
        sa2 = E.c("cr_sa2")
        vt2 = E.c("cr_vt2")
        va = E.c("cr_va")
        gmm = E.c("cr_gmm")
        sgv = E.c("cr_sgv")
        vzt = E.c("cr_vzt")
        ztp = E.c("cr_ztp")
        zec = E.c("cr_zec")
        zspl = E.c("cr_zspl")
        (wld0, ald0, qld0, crm0, cre0, vaf0, sa20, vt20, gmm0, sgv0, vzt0,
         ztp0, zec0, zspl0) = (wld.n, ald.n, qld.n, crm.n, cre.n, vaf.n, sa2.n,
                               vt2.n, gmm.n, sgv.n, vzt.n, ztp.n, zec.n, zspl.n)

        @E.block.gpsimd
        def _(g):
            E.c("ph").wait(g, phn)
            for i in range(4):
                wld.inc(g.dma_start(
                    out=wgt[:, i * 4 * D:(i + 1) * 4 * D].rearrange("p (c o) -> p c o", o=D),
                    in_=E.d["wgate_t"][i * 512:(i + 1) * 512, :]
                    .rearrange("(c p) o -> p c o", p=128)), 16)
            for tc in range(NTC):
                if tc >= 1:
                    crm.wait(g, crm0 + tc * H)
                qld.inc(g.dma_start(
                    out=qtl[:].rearrange("p (h d) -> p h d", d=128),
                    in_=E.d["q_s"][:, tc].rearrange("h p d -> p h d")), 16)

        @E.block.sync
        def _(sync):
            E.c("ph").wait(sync, phn)

            def spill_tile(tc):
                for fc in range(NFC):
                    k = tc * NFC + fc
                    zec.wait(sync, zec0 + k + 1)
                    zspl.inc(sync.dma_start(
                        out=E.d["z_s"][tc, fc * 128:(fc + 1) * 128, :],
                        in_=zev[k % 4][:]), 16)

            for tc in range(NTC):
                if tc >= 2:
                    vaf.wait(sync, vaf0 + tc - 1)
                ald.inc(sync.dma_start(
                    out=atile[tc % 2][:],
                    in_=E.d["attn_s"][tc * 128:(tc + 1) * 128, :]), 16)
                if tc >= 1:
                    spill_tile(tc - 1)
            spill_tile(NTC - 1)

        @E.block.tensor
        def _(t):
            E.c("ph").wait(t, phn)
            ctr, n = E.sroute_done
            ctr.wait(t, n)
            for tc in range(NTC):
                qld.wait(t, qld0 + 16 * (tc + 1))
                for h in range(H):
                    r = tc * H + h
                    if r >= 2:
                        cre.wait(t, cre0 + r - 1)
                    crm.inc(t.matmul(psR[r % 2][:, 0:128],
                                     qtl[:, h * 128:(h + 1) * 128],
                                     E.t["Sbf"][:, h * 128:(h + 1) * 128],
                                     start=True, stop=True))
                if tc == 0:
                    wld.wait(t, wld0 + 64)
                for og in range(4):
                    z = tc * 4 + og
                    if z >= 2:
                        sgv.wait(t, sgv0 + z - 1)
                    for fc in range(NFC):
                        ins = t.matmul(
                            psG[z % 2][:],
                            E.t["xnf"][:, fc * TOK + tc * 128: fc * TOK + (tc + 1) * 128],
                            wgt[:, fc * D + og * 512: fc * D + (og + 1) * 512],
                            start=(fc == 0), stop=(fc == NFC - 1))
                    gmm.inc(ins)
                for fc in range(NFC):
                    k = tc * NFC + fc
                    vzt.wait(t, vzt0 + tc * 4 + fc // 4 + 1)
                    if k >= 2:
                        zec.wait(t, zec0 + k - 1)
                    ztp.inc(t.transpose(psT[k % 2][:, 0:128],
                                        ztm[tc % 2][:, fc * 128:(fc + 1) * 128],
                                        E.t["ident"][:]))

        @E.block.scalar
        def _(s):
            E.c("ph").wait(s, phn)
            for tc in range(NTC):
                for h in range(H):
                    r = tc * H + h
                    crm.wait(s, crm0 + r + 1)
                    if h == 0 and tc >= 1:
                        vaf.wait(s, vaf0 + tc)
                    cre.inc(s.activation(
                        out=corr[:, h * 128:(h + 1) * 128],
                        in_=psR[r % 2][:, 0:128], func=AF.Copy,
                        scale=E.t["qdecg_t"][:, h * 16 + tc:h * 16 + tc + 1]))
                vaf.wait(s, vaf0 + tc + 1)
                sa2.inc(s.activation(out=corr[:], in_=af[:], func=AF.Square,
                                     accum_out=ssq[:]))
                sa2.inc(s.activation(out=sd[:], in_=ssq[:], func=AF.Sqrt,
                                     scale=1.0 / D, bias=E.t["cc"][:, 1:2]))
                for og in range(4):
                    z = tc * 4 + og
                    gmm.wait(s, gmm0 + z + 1)
                    if z >= 2:
                        vzt.wait(s, vzt0 + z - 1)
                    sgv.inc(s.activation(out=sg[z % 2][:], in_=psG[z % 2][:],
                                         func=AF.Sigmoid))

        @E.block.vector
        def _(v):
            E.c("ph").wait(v, phn)
            for tc in range(NTC):
                ald.wait(v, ald0 + 16 * (tc + 1))
                cre.wait(v, cre0 + (tc + 1) * H)
                if tc >= 1:
                    sa2.wait(v, sa20 + 2 * tc - 1)
                vaf.inc(v.tensor_tensor(out=af[:], in0=atile[tc % 2][:],
                                        in1=corr[:], op=ALU.add))
                sa2.wait(v, sa20 + 2 * (tc + 1))
                va.inc(v.reciprocal(rstd[:], sd[:]))
                va.wait(v)
                vt2.inc(v.scalar_tensor_tensor(out=atile[tc % 2][:], in0=af[:],
                                               scalar=rstd[:], in1=E.t["normw_b"][:],
                                               op0=ALU.mult, op1=ALU.mult))
                for og in range(4):
                    z = tc * 4 + og
                    sgv.wait(v, sgv0 + z + 1)
                    if og == 0 and tc >= 2:
                        ztp.wait(v, ztp0 + (tc - 1) * NFC)
                    vzt.inc(v.tensor_tensor(
                        out=ztm[tc % 2][:, og * 512:(og + 1) * 512],
                        in0=atile[tc % 2][:, og * 512:(og + 1) * 512], in1=sg[z % 2][:],
                        op=ALU.mult))
                for fc in range(NFC):
                    k = tc * NFC + fc
                    ztp.wait(v, ztp0 + k + 1)
                    if k >= 4:
                        zspl.wait(v, zspl0 + 16 * (k - 3))
                    zec.inc(v.tensor_copy(out=zev[k % 4][:], in_=psT[k % 2][:, 0:128]))

        zspl.n = zspl.n
        E.crz_done = (zspl, zspl0 + 16 * NTC * NFC)
        zspl.n = zspl0 + 16 * NTC * NFC



def emit_ffn_gu(E, layer):
    nc = E.nc
    phn = E.c("ph").n
    NHC = FFN // 128
    with ExitStack() as phs:
        e = phs.enter_context
        wg = [e(nc.sbuf_tensor(f"fg_wg{i}_{layer}", [128, NFC * 128], BF16)) for i in range(2)]
        wu = [e(nc.sbuf_tensor(f"fg_wu{i}_{layer}", [128, NFC * 128], BF16)) for i in range(2)]
        sgt = [e(nc.sbuf_tensor(f"fg_sg{i}_{layer}", [128, 512], F32)) for i in range(2)]
        gtile = [e(nc.sbuf_tensor(f"fg_g{i}_{layer}", [128, 512], BF16)) for i in range(4)]
        psGg = [e(nc.psum_tensor(f"fg_psG{i}_{layer}", [128, 512], F32)) for i in range(2)]
        psU = [e(nc.psum_tensor(f"fg_psU{i}_{layer}", [128, 512], F32)) for i in range(2)]

        wld = E.c("fg_wld")
        mgu = E.c("fg_mgu")
        sgv = E.c("fg_sgv")
        vgg = E.c("fg_vgg")
        gspl = E.c("fg_gspl")
        wld0, mgu0, sgv0, vgg0, gspl0 = wld.n, mgu.n, sgv.n, vgg.n, gspl.n

        if layer == "l0":
            srcs = (E.d["w0g_t"], E.d["w0u_t"])
        else:
            srcs = (E.d["w1g"], E.d["w1u"])

        @E.block.gpsimd
        def _(g):
            E.c("ph").wait(g, phn)
            for hc in range(NHC):
                if hc >= 2:
                    mgu.wait(g, mgu0 + (hc - 1) * 8)
                for w_sb, src in ((wg, srcs[0]), (wu, srcs[1])):
                    wld.inc(g.dma_start(
                        out=w_sb[hc % 2][:].rearrange("p (c o) -> p c o", o=128),
                        in_=src[:, hc * 128:(hc + 1) * 128]
                        .rearrange("(c p) o -> p c o", p=128)), 16)

        @E.block.tensor
        def _(t):
            E.c("ph").wait(t, phn)
            for hc in range(NHC):
                wld.wait(t, wld0 + 32 * (hc + 1))
                for tg in range(4):
                    z = hc * 4 + tg
                    if z >= 2:
                        sgv.wait(t, sgv0 + z - 1)
                    for fc in range(NFC):
                        ins = t.matmul(
                            psGg[z % 2][:], wg[hc % 2][:, fc * 128:(fc + 1) * 128],
                            E.t["xnf"][:, fc * TOK + tg * 512: fc * TOK + (tg + 1) * 512],
                            start=(fc == 0), stop=(fc == NFC - 1))
                    mgu.inc(ins)
                    if z >= 2:
                        vgg.wait(t, vgg0 + z - 1)
                    for fc in range(NFC):
                        ins = t.matmul(
                            psU[z % 2][:], wu[hc % 2][:, fc * 128:(fc + 1) * 128],
                            E.t["xnf"][:, fc * TOK + tg * 512: fc * TOK + (tg + 1) * 512],
                            start=(fc == 0), stop=(fc == NFC - 1))
                    mgu.inc(ins)

        @E.block.scalar
        def _(s):
            E.c("ph").wait(s, phn)
            for hc in range(NHC):
                for tg in range(4):
                    z = hc * 4 + tg
                    mgu.wait(s, mgu0 + z * 2 + 1)
                    if z >= 2:
                        vgg.wait(s, vgg0 + z - 1)
                    if layer == "l0":
                        sgv.inc(s.activation(out=sgt[z % 2][:], in_=psGg[z % 2][:],
                                             func=AF.Silu))
                    else:
                        sgv.inc(s.activation(out=sgt[z % 2][:], in_=psGg[z % 2][:],
                                             func=AF.Silu,
                                             scale=E.t["s1g"][:, hc:hc + 1]))

        @E.block.vector
        def _(v):
            E.c("ph").wait(v, phn)
            for hc in range(NHC):
                for tg in range(4):
                    z = hc * 4 + tg
                    mgu.wait(v, mgu0 + z * 2 + 2)
                    sgv.wait(v, sgv0 + z + 1)
                    if z >= 4:
                        gspl.wait(v, gspl0 + 16 * (z - 3))
                    if layer == "l0":
                        vgg.inc(v.tensor_tensor(out=gtile[z % 4][:], in0=sgt[z % 2][:],
                                                in1=psU[z % 2][:], op=ALU.mult))
                    else:
                        vgg.inc(v.scalar_tensor_tensor(
                            out=gtile[z % 4][:], in0=psU[z % 2][:],
                            scalar=E.t["s1u"][:, hc:hc + 1], in1=sgt[z % 2][:],
                            op0=ALU.mult, op1=ALU.mult))

        @E.block.sync
        def _(sync):
            for hc in range(NHC):
                for tg in range(4):
                    z = hc * 4 + tg
                    vgg.wait(sync, vgg0 + z + 1)
                    gspl.inc(sync.dma_start(
                        out=E.d["g_s"][tg, hc * 128:(hc + 1) * 128, :],
                        in_=gtile[z % 4][:]), 16)
        E.gu_done = (gspl, gspl.n)



def emit_ffn_down(E, layer):
    nc = E.nc
    phn = E.c("ph").n
    NHC = FFN // 128
    with ExitStack() as phs:
        e = phs.enter_context
        wd = e(nc.sbuf_tensor(f"fdw_{layer}", [128, 64 * 512], BF16))
        gb = [e(nc.sbuf_tensor(f"fd_gb{i}_{layer}", [128, 64 * 128], BF16)) for i in range(2)]
        res1 = [e(nc.sbuf_tensor(f"fd_r1{i}_{layer}", [128, 512], F32)) for i in range(2)]
        osb = [e(nc.sbuf_tensor(f"fd_o{i}_{layer}", [128, 512], F32)) for i in range(4)]
        if layer == "l1":
            res2 = [e(nc.sbuf_tensor(f"fd_r2{i}_{layer}", [128, 512], F32)) for i in range(2)]
            tmp = e(nc.sbuf_tensor(f"fd_tmp_{layer}", [128, 512], F32))
            s1d = e(nc.sbuf_tensor("fd_s1d", [128, D], F32))
        psD = [e(nc.psum_tensor(f"fd_psD{i}_{layer}", [128, 512], F32)) for i in range(2)]

        wld = E.c("fd_wld")
        gld = E.c("fd_gld")
        rld = E.c("fd_rld")
        mdn = E.c("fd_mdn")
        vdn = E.c("fd_vdn")
        ospl = E.c("fd_ospl")
        wld0, gld0, rld0, mdn0, vdn0, ospl0 = wld.n, gld.n, rld.n, mdn.n, vdn.n, ospl.n
        wsrc = E.d["w0d_t"] if layer == "l0" else E.d["w1d"]
        rper = 1 if layer == "l0" else 2

        @E.block.gpsimd
        def _(g):
            E.c("ph").wait(g, phn)
            for dblk in range(4):
                if dblk >= 1:
                    mdn.wait(g, mdn0 + dblk * 16)
                for grp in range(8):
                    wld.inc(g.dma_start(
                        out=wd[:, grp * 8 * 512:(grp + 1) * 8 * 512]
                        .rearrange("p (c o) -> p c o", o=512),
                        in_=wsrc[grp * 1024:(grp + 1) * 1024,
                                 dblk * 512:(dblk + 1) * 512]
                        .rearrange("(c p) o -> p c o", p=128)), 16)

        @E.block.sync
        def _(sync):
            E.c("ph").wait(sync, phn)
            dst = E.d["x1p_s"] if layer == "l0" else E.d["out"]
            if layer == "l1":
                rld.inc(sync.dma_start(out=s1d[:], in_=E.d["s1d_b"]), 16)

            def spill_u(u):
                tcs = (u % 16) // 4 * 4 + u % 4
                dbs = u // 16
                vdn.wait(sync, vdn0 + u + 1)
                ospl.inc(sync.dma_start(
                    out=dst[tcs * 128:(tcs + 1) * 128, dbs * 512:(dbs + 1) * 512],
                    in_=osb[u % 4][:]), 16)

            for dblk in range(4):
                for tg in range(4):
                    for tl in range(4):
                        u = dblk * 16 + tg * 4 + tl
                        tc = tg * 4 + tl
                        if u >= 2:
                            mdn.wait(sync, mdn0 + u - 1)
                            vdn.wait(sync, vdn0 + u - 1)
                        for grp in range(4):
                            gld.inc(sync.dma_start(
                                out=gb[u % 2][:, grp * 16 * 128:(grp + 1) * 16 * 128]
                                .rearrange("p (c t) -> p c t", t=128),
                                in_=E.d["g_s"][tg, grp * 2048:(grp + 1) * 2048,
                                               tl * 128:(tl + 1) * 128]
                                .rearrange("(c p) t -> p c t", p=128)), 16)
                        if layer == "l0":
                            rld.inc(sync.dma_start(
                                out=res1[u % 2][:],
                                in_=E.d["x"][tc * 128:(tc + 1) * 128,
                                             dblk * 512:(dblk + 1) * 512]), 16)
                        else:
                            rld.inc(sync.dma_start(
                                out=res1[u % 2][:],
                                in_=E.d["x1_s"][tc * 128:(tc + 1) * 128,
                                                dblk * 512:(dblk + 1) * 512]), 16)
                            rld.inc(sync.dma_start(
                                out=res2[u % 2][:],
                                in_=E.d["y_s"][tc, :, dblk * 512:(dblk + 1) * 512]), 16)
                        if u >= 1:
                            spill_u(u - 1)
            spill_u(63)

        @E.block.tensor
        def _(t):
            E.c("ph").wait(t, phn)
            for dblk in range(4):
                wld.wait(t, wld0 + 16 * 8 * (dblk + 1))
                for tg in range(4):
                    for tl in range(4):
                        u = dblk * 16 + tg * 4 + tl
                        gld.wait(t, gld0 + 64 * (u + 1))
                        if u >= 2:
                            vdn.wait(t, vdn0 + u - 1)
                        for hc in range(NHC):
                            ins = t.matmul(psD[u % 2][:],
                                           gb[u % 2][:, hc * 128:(hc + 1) * 128],
                                           wd[:, hc * 512:(hc + 1) * 512],
                                           start=(hc == 0), stop=(hc == NHC - 1))
                        mdn.inc(ins)

        @E.block.vector
        def _(v):
            E.c("ph").wait(v, phn)
            for dblk in range(4):
                for tg in range(4):
                    for tl in range(4):
                        u = dblk * 16 + tg * 4 + tl
                        mdn.wait(v, mdn0 + u + 1)
                        rld.wait(v, rld0 + 16 * rper * (u + 1) + (16 if layer == "l1" else 0))
                        if u >= 4:
                            ospl.wait(v, ospl0 + 16 * (u - 3))
                        if layer == "l0":
                            vdn.inc(v.tensor_tensor(out=osb[u % 4][:], in0=psD[u % 2][:],
                                                    in1=res1[u % 2][:], op=ALU.add))
                        else:
                            v.tensor_tensor(out=tmp[:], in0=psD[u % 2][:],
                                            in1=s1d[:, dblk * 512:(dblk + 1) * 512],
                                            op=ALU.mult)
                            v.tensor_tensor(out=tmp[:], in0=tmp[:], in1=res1[u % 2][:],
                                            op=ALU.add)
                            vdn.inc(v.tensor_tensor(out=osb[u % 4][:], in0=tmp[:],
                                                    in1=res2[u % 2][:], op=ALU.add))

        E.down_done = (ospl, ospl0 + 16 * 64)
        ospl.n = ospl0 + 16 * 64



def emit_outproj(E):
    nc = E.nc
    phn = E.c("ph").n
    with ExitStack() as phs:
        e = phs.enter_context
        wo = e(nc.sbuf_tensor("op_wo", [128, NFC * D], BF16))
        zf = [e(nc.sbuf_tensor(f"op_zf{i}", [128, NFC * 128], BF16)) for i in range(2)]
        xp = [e(nc.sbuf_tensor(f"op_xp{i}", [128, D], F32)) for i in range(2)]
        x1sb = [e(nc.sbuf_tensor(f"op_x1{i}", [128, D], F32)) for i in range(2)]
        psO = [e(nc.psum_tensor(f"op_psO{i}", [128, 512], F32)) for i in range(2)]

        wld = E.c("op_wld")
        zld = E.c("op_zld")
        mop = E.c("op_mop")
        vop = E.c("op_vop")
        xspl = E.c("op_xspl")
        wld0, zld0, mop0, vop0, xspl0 = wld.n, zld.n, mop.n, vop.n, xspl.n

        @E.block.gpsimd
        def _(g):
            E.c("ph").wait(g, phn)
            for i in range(4):
                wld.inc(g.dma_start(
                    out=wo[:, i * 4 * D:(i + 1) * 4 * D].rearrange("p (c o) -> p c o", o=D),
                    in_=E.d["wout_t"][i * 512:(i + 1) * 512, :]
                    .rearrange("(c p) o -> p c o", p=128)), 16)

        @E.block.sync
        def _(sync):
            E.c("ph").wait(sync, phn)

            def spill_tile(tc):
                vop.wait(sync, vop0 + 4 * (tc + 1))
                xspl.inc(sync.dma_start(
                    out=E.d["x1_s"][tc * 128:(tc + 1) * 128, :],
                    in_=x1sb[tc % 2][:]), 16)

            for tc in range(NTC):
                if tc >= 2:
                    mop.wait(sync, mop0 + (tc - 1) * 4)
                    vop.wait(sync, vop0 + (tc - 1) * 4)
                zld.inc(sync.dma_start(
                    out=zf[tc % 2][:].rearrange("p (c t) -> p c t", t=128),
                    in_=E.d["z_s"][tc].rearrange("(c p) t -> p c t", p=128)), 16)
                zld.inc(sync.dma_start(
                    out=xp[tc % 2][:],
                    in_=E.d["x1p_s"][tc * 128:(tc + 1) * 128, :]), 16)
                if tc >= 1:
                    spill_tile(tc - 1)
            spill_tile(NTC - 1)

        @E.block.tensor
        def _(t):
            E.c("ph").wait(t, phn)
            wld.wait(t, wld0 + 64)
            for tc in range(NTC):
                zld.wait(t, zld0 + 32 * (tc + 1))
                for og in range(4):
                    w = tc * 4 + og
                    if w >= 2:
                        vop.wait(t, vop0 + w - 1)
                    for fc in range(NFC):
                        ins = t.matmul(
                            psO[w % 2][:], zf[tc % 2][:, fc * 128:(fc + 1) * 128],
                            wo[:, fc * D + og * 512: fc * D + (og + 1) * 512],
                            start=(fc == 0), stop=(fc == NFC - 1))
                    mop.inc(ins)

        @E.block.vector
        def _(v):
            E.c("ph").wait(v, phn)
            for tc in range(NTC):
                for og in range(4):
                    w = tc * 4 + og
                    mop.wait(v, mop0 + w + 1)
                    if og == 0 and tc >= 2:
                        xspl.wait(v, xspl0 + 16 * (tc - 1))
                    vop.inc(v.tensor_tensor(
                        out=x1sb[tc % 2][:, og * 512:(og + 1) * 512],
                        in0=psO[w % 2][:], in1=xp[tc % 2][:, og * 512:(og + 1) * 512],
                        op=ALU.add))

        E.op_done = (xspl, xspl0 + 16 * NTC)
        xspl.n = xspl0 + 16 * NTC



def emit_l1_gates_scan(E):
    nc = E.nc
    phn = E.c("ph").n
    with ExitStack() as phs:
        e = phs.enter_context
        at = [e(nc.sbuf_tensor(f"g1_a{i}", [128, TOK], F32)) for i in range(2)]
        nxt = [e(nc.sbuf_tensor(f"g1_nx{i}", [128, TOK], F32)) for i in range(2)]
        yl = e(nc.sbuf_tensor("g1_yl", [128, TOK], F32))
        gx = [e(nc.sbuf_tensor(f"g1_gx{i}", [128, 512], F32)) for i in range(2)]
        ga = [e(nc.sbuf_tensor(f"g1_ga{i}", [128, 512], F32)) for i in range(2)]
        la = [e(nc.sbuf_tensor(f"g1_la{i}", [128, 512], F32)) for i in range(2)]
        e2 = [e(nc.sbuf_tensor(f"g1_e2{i}", [128, 512], F32)) for i in range(2)]
        om = [e(nc.sbuf_tensor(f"g1_om{i}", [128, 512], F32)) for i in range(2)]
        ml = [e(nc.sbuf_tensor(f"g1_ml{i}", [128, 512], F32)) for i in range(2)]
        gxx = e(nc.sbuf_tensor("g1_gxx", [128, 512], F32))
        S1sb = e(nc.sbuf_tensor("g1_S1", [128, H], F32))
        psX = [e(nc.psum_tensor(f"g1_psX{i}", [128, 512], F32)) for i in range(2)]
        psA_ = [e(nc.psum_tensor(f"g1_psA{i}", [128, 512], F32)) for i in range(2)]

        mgx = E.c("g1_mgx")
        sg1 = E.c("g1_sg1")
        v1 = E.c("g1_v1")
        s2 = E.c("g1_s2")
        v2 = E.c("g1_v2")
        s3 = E.c("g1_s3")
        vml = E.c("g1_vml")
        v3 = E.c("g1_v3")
        vsc = E.c("g1_vsc")
        ss1 = E.c("g1_ss1")
        aspl = E.c("g1_aspl")
        (mgx0, sg10, v10, s20, v20, s30, vml0, v30, vsc0, ss10, aspl0) = (
            mgx.n, sg1.n, v1.n, s2.n, v2.n, s3.n, vml.n, v3.n, vsc.n, ss1.n, aspl.n)

        @E.block.tensor
        def _(t):
            E.c("ph").wait(t, phn)
            for hc in range(H):
                for tg in range(4):
                    z = hc * 4 + tg
                    if z >= 2:
                        sg1.wait(t, sg10 + z - 1)
                    t.matmul(psX[z % 2][:], E.t["igw"][:, hc * 128:(hc + 1) * 128],
                             E.t["xnf"][:, hc * TOK + tg * 512: hc * TOK + (tg + 1) * 512],
                             start=True, stop=True)
                    ins = t.matmul(psA_[z % 2][:], E.t["agw"][:, hc * 128:(hc + 1) * 128],
                                   E.t["xnf"][:, hc * TOK + tg * 512: hc * TOK + (tg + 1) * 512],
                                   start=True, stop=True)
                    mgx.inc(ins, 2)

        @E.block.scalar
        def _(s):
            E.c("ph").wait(s, phn)
            for hc in range(H):
                for tg in range(4):
                    z = hc * 4 + tg
                    mgx.wait(s, mgx0 + 2 * (z + 1))
                    if z >= 2:
                        v1.wait(s, v10 + z - 1)
                        v3.wait(s, v30 + z - 1)
                    s.activation(out=gx[z % 2][:], in_=psX[z % 2][:], func=AF.Sigmoid,
                                 bias=E.t["igb"][:, hc:hc + 1])
                    sg1.inc(s.activation(out=ga[z % 2][:], in_=psA_[z % 2][:],
                                         func=AF.Sigmoid, bias=E.t["agb"][:, hc:hc + 1]))
                    v1.wait(s, v10 + z + 1)
                    if hc >= 2 and tg == 0:
                        vsc.wait(s, vsc0 + hc - 1)
                        aspl.wait(s, aspl0 + 32 * (hc - 1))
                    s.activation(out=at[hc % 2][:, tg * 512:(tg + 1) * 512],
                                 in_=la[z % 2][:], func=AF.Exp)
                    s2.inc(s.activation(out=e2[z % 2][:], in_=la[z % 2][:],
                                        func=AF.Exp, scale=2.0))
                    v2.wait(s, v20 + z + 1)
                    s3.inc(s.activation(out=ml[z % 2][:], in_=om[z % 2][:], func=AF.Sqrt))
                if hc >= 1:
                    vsc.wait(s, vsc0 + hc)
                    ss1.inc(s.copy(out=S1sb[:, hc - 1:hc], in_=yl[:, TOK - 1:TOK]))

        @E.block.vector
        def _(v):
            E.c("ph").wait(v, phn)
            for hc in range(H):
                for tg in range(4):
                    z = hc * 4 + tg
                    sg1.wait(v, sg10 + z + 1)
                    if z >= 2:
                        s2.wait(v, s20 + z - 1)
                    v1.inc(v.tensor_scalar_mul(out=la[z % 2][:], in0=ga[z % 2][:],
                                               scalar1=E.t["m8spa"][:, hc:hc + 1]))
                    s2.wait(v, s20 + z + 1)
                    if z >= 2:
                        s3.wait(v, s30 + z - 1)
                    v2.inc(v.tensor_scalar(out=om[z % 2][:], in0=e2[z % 2][:],
                                           scalar1=-1.0, scalar2=1.0,
                                           op0=ALU.mult, op1=ALU.add))
                    s3.wait(v, s30 + z + 1)
                    if tg == 0:
                        vml.inc(v.tensor_scalar(out=ml[z % 2][:, 0:1], in0=ml[z % 2][:, 0:1],
                                                scalar1=E.t["omrcol"][:],
                                                scalar2=E.t["rcol"][:],
                                                op0=ALU.mult, op1=ALU.add))
                        vml.wait(v)
                    v.tensor_tensor(out=gxx[:],
                                    in0=E.t["xnf"][:, hc * TOK + tg * 512: hc * TOK + (tg + 1) * 512],
                                    in1=gx[z % 2][:], op=ALU.mult)
                    if hc >= 2 and tg == 0:
                        aspl.wait(v, aspl0 + 32 * (hc - 1))
                    v3.inc(v.tensor_tensor(out=nxt[hc % 2][:, tg * 512:(tg + 1) * 512],
                                           in0=gxx[:], in1=ml[z % 2][:], op=ALU.mult))
                s2.wait(v, s20 + (hc * 4 + 4))
                if hc >= 1:
                    ss1.wait(v, ss10 + hc)
                vsc.inc(v.tensor_tensor_scan(out=yl[:], data0=at[hc % 2][:],
                                             data1=nxt[hc % 2][:], initial=0.0,
                                             op0=ALU.mult, op1=ALU.add))

        @E.block.scalar
        def _(s):
            vsc.wait(s, vsc0 + H)
            ss1.inc(s.copy(out=S1sb[:, H - 1:H], in_=yl[:, TOK - 1:TOK]))

        @E.block.sync
        def _(sync):
            for hc in range(H):
                s2.wait(sync, s20 + hc * 4 + 4)
                v3.wait(sync, v30 + (hc + 1) * 4)
                aspl.inc(sync.dma_start(out=E.d["a_s"][hc], in_=at[hc % 2][:]), 16)
                aspl.inc(sync.dma_start(out=E.d["nx_s"][hc], in_=nxt[hc % 2][:]), 16)
            ss1.wait(sync, ss10 + H)
            aspl.inc(sync.dma_start(out=E.d["S1_out"], in_=S1sb[:]), 16)
        E.g1_done = (aspl, aspl.n)



def emit_s1route(E, src_ap, cc_wait=None):
    nc = E.nc
    st = E.stack.enter_context
    E.t["S1i"] = st(nc.sbuf_tensor("S1i", [128, H], F32))
    phn = E.c("ph").n
    ld = E.c("ld_s1route")
    srt = E.c("s1rt")

    @E.block.sync
    def _(sync):
        E.c("ph").wait(sync, phn)
        if cc_wait is not None:
            cc_wait[0].wait(sync, cc_wait[1])
        ld.inc(sync.dma_start(out=E.t["S1i"][:], in_=src_ap), 16)

    @E.block.vector
    def _(v):
        E.c("ph").wait(v, phn)
        ld.wait(v)
        srt.inc(v.tensor_scalar_mul(out=E.t["S1i"][:], in0=E.t["S1i"][:],
                                    scalar1=E.t["selp"][:]))
    E.s1route_done = (srt, srt.n)



def emit_scan2_y(E):
    nc = E.nc
    phn = E.c("ph").n
    with ExitStack() as phs:
        e = phs.enter_context
        a2 = [e(nc.sbuf_tensor(f"s2_a{i}", [128, TOK], F32)) for i in range(2)]
        nx2 = [e(nc.sbuf_tensor(f"s2_nx{i}", [128, TOK], F32)) for i in range(2)]
        yf = [e(nc.sbuf_tensor(f"s2_y{i}", [128, TOK], F32)) for i in range(2)]
        yev = [e(nc.sbuf_tensor(f"s2_ye{i}", [128, 128], F32)) for i in range(4)]
        psY = [e(nc.psum_tensor(f"s2_psY{i}", [128, 512], F32)) for i in range(2)]

        ld2 = E.c("s2_ld")
        vs2 = E.c("s2_vs2")
        ytp = E.c("s2_ytp")
        yec = E.c("s2_yec")
        yspl = E.c("s2_yspl")
        ld20, vs20, ytp0, yec0, yspl0 = ld2.n, vs2.n, ytp.n, yec.n, yspl.n

        @E.block.sync
        def _(sync):
            E.c("ph").wait(sync, phn)

            def spill_hc(hc):
                for tc in range(NTC):
                    k = hc * NTC + tc
                    yec.wait(sync, yec0 + k + 1)
                    yspl.inc(sync.dma_start(
                        out=E.d["y_s"][tc, :, hc * 128:(hc + 1) * 128],
                        in_=yev[k % 4][:]), 16)

            for hc in range(H):
                if hc >= 2:
                    vs2.wait(sync, vs20 + hc - 1)
                ld2.inc(sync.dma_start(out=a2[hc % 2][:], in_=E.d["a_s"][hc]), 16)
                ld2.inc(sync.dma_start(out=nx2[hc % 2][:], in_=E.d["nx_s"][hc]), 16)
                if hc >= 1:
                    spill_hc(hc - 1)
            spill_hc(H - 1)

        @E.block.vector
        def _(v):
            E.c("ph").wait(v, phn)
            ctr, n = E.s1route_done
            ctr.wait(v, n)
            for hc in range(H):
                ld2.wait(v, ld20 + 32 * (hc + 1))
                if hc >= 2:
                    ytp.wait(v, ytp0 + (hc - 1) * NTC)
                vs2.inc(v.tensor_tensor_scan(out=yf[hc % 2][:], data0=a2[hc % 2][:],
                                             data1=nx2[hc % 2][:],
                                             initial=E.t["S1i"][:, hc:hc + 1],
                                             op0=ALU.mult, op1=ALU.add))
                for tc in range(NTC):
                    k = hc * NTC + tc
                    ytp.wait(v, ytp0 + k + 1)
                    if k >= 4:
                        yspl.wait(v, yspl0 + 16 * (k - 3))
                    yec.inc(v.tensor_copy(out=yev[k % 4][:], in_=psY[k % 2][:, 0:128]))

        @E.block.tensor
        def _(t):
            E.c("ph").wait(t, phn)
            for hc in range(H):
                vs2.wait(t, vs20 + hc + 1)
                for tc in range(NTC):
                    k = hc * NTC + tc
                    if k >= 2:
                        yec.wait(t, yec0 + k - 1)
                    ytp.inc(t.transpose(psY[k % 2][:, 0:128],
                                        yf[hc % 2][:, tc * 128:(tc + 1) * 128],
                                        E.t["identf"][:]))

        E.s2_done = (yspl, yspl0 + 16 * H * NTC)
        yspl.n = yspl0 + 16 * H * NTC



def emit_xnf_load(E, src_ap):
    nc = E.nc
    st = E.stack.enter_context
    if "xnf" not in E.t:
        E.t["xnf"] = st(nc.sbuf_tensor("xnf", [128, NFC * TOK], BF16))
    ld = E.c("ld_xnf")

    @E.block.sync
    def _(sync):
        ld.inc(sync.dma_start(out=E.t["xnf"][:], in_=src_ap), 16)
    E.xnf_done = (ld, ld.n)


def emit_xnf_spill(E, dst_ap):
    sp = E.c("sp_xnf")
    phn = E.c("ph").n

    @E.block.sync
    def _(sync):
        E.c("ph").wait(sync, phn)
        sp.inc(sync.dma_start(out=dst_ap, in_=E.t["xnf"][:]), 16)
    E.xnfspill_done = (sp, sp.n)



def _setup_f32_ident(E):
    nc = E.nc
    E.t["identf"] = E.stack.enter_context(nc.sbuf_tensor("identf", [128, 128], F32))
    gpf = E.c("gp_identf")

    @E.block.gpsimd
    def _(g):
        g.memset(E.t["identf"][:], 0)
        gpf.inc(g.affine_select(out=E.t["identf"][:], in_=E.t["identf"][:],
                                compare_op=ALU.not_equal, fill=1.0, base=0,
                                pattern=[[-1, 128]], channel_multiplier=1))
    return gpf


def build_B():
    nc = bass.Bass("TRN2", target_bir_lowering=False, debug=False, detect_race_conditions=False)
    d = {}
    for name, shape, dt in [
            ("x", [TOK, D], F32), ("slope_c", [128, H], F32), ("selp", [128, 1], F32),
            ("wgate_t", [D, D], F32), ("wout_t", [D, D], F32), ("normw_b", [128, D], F32),
            ("w0g_t", [D, FFN], F32), ("w0u_t", [D, FFN], F32), ("w0d_t", [FFN, D], F32),
            ("igw", [H, 128, 128], F32), ("agw", [H, 128, 128], F32),
            ("igb_c", [128, H], F32), ("agb_c", [128, H], F32), ("apar_c", [128, H], F32),
            ("rcol", [128, 1], F32), ("omrcol", [128, 1], F32),
            ("attn_s", [TOK, D], F32), ("q_s", [H, NTC, 128, 128], BF16),
            ("S_in", [D, DH], F32)]:
        d[name] = nc.dram_tensor(name, shape, dt, kind="ExternalInput").ap()
    for name, shape, dt in [
            ("x1_s", [TOK, D], F32), ("S1_out", [128, H], F32),
            ("a_s", [H, 128, TOK], F32), ("nx_s", [H, 128, TOK], F32),
            ("xn1_s", [128, NFC * TOK], BF16)]:
        d[name] = nc.dram_tensor(name, shape, dt, kind="ExternalOutput").ap()
    for name, shape, dt in [
            ("x1p_s", [TOK, D], F32), ("g_s", [4, FFN, 512], BF16),
            ("z_s", [NTC, D, 128], BF16)]:
        d[name] = nc.dram_tensor(name, shape, dt, kind="Internal").ap()
    with ExitStack() as stack:
        block = stack.enter_context(nc.Block())
        E = Emit(nc, stack, block)
        E.d = d
        ph = E.c("ph")
        emit_setup(E, "B")
        emit_setup2(E, "B")
        scalar_fence(E, [(E.setup_done, E.setup_done.n), E.setup2_done,
                         E.setup2g_done], ph)
        emit_ln(E, d["x"], "l0")
        scalar_fence(E, [E.ln_done], ph)
        emit_sroute(E, d["S_in"])
        emit_corr_rms_gate_z(E)
        scalar_fence(E, [E.crz_done], ph)
        emit_ffn_gu(E, "l0")
        scalar_fence(E, [E.gu_done], ph)
        emit_ffn_down(E, "l0")
        scalar_fence(E, [E.down_done], ph)
        emit_outproj(E)
        scalar_fence(E, [E.op_done], ph)
        emit_ln(E, d["x1_s"], "l1")
        scalar_fence(E, [E.ln_done], ph)
        emit_xnf_spill(E, d["xn1_s"])
        emit_l1_gates_scan(E)

        @E.block.gpsimd
        def _(g):
            ctr, n = E.g1_done
            ctr.wait(g, n)
            ctr, n = E.xnfspill_done
            ctr.wait(g, n)
    return nc


def build_C():
    nc = bass.Bass("TRN2", target_bir_lowering=False, debug=False, detect_race_conditions=False)
    d = {}
    for name, shape, dt in [
            ("slope_c", [128, H], F32), ("selp", [128, 1], F32),
            ("x1_s", [TOK, D], F32), ("S1_in", [128, H], F32),
            ("a_s", [H, 128, TOK], F32), ("nx_s", [H, 128, TOK], F32),
            ("xn1_s", [128, NFC * TOK], BF16),
            ("w1g", [D, FFN], I8), ("w1u", [D, FFN], I8), ("w1d", [FFN, D], I8),
            ("s1g_c", [128, 64], F32), ("s1u_c", [128, 64], F32), ("s1d_b", [128, D], F32)]:
        d[name] = nc.dram_tensor(name, shape, dt, kind="ExternalInput").ap()
    d["out"] = nc.dram_tensor("out", [TOK, D], F32, kind="ExternalOutput").ap()
    for name, shape, dt in [
            ("g_s", [4, FFN, 512], BF16), ("y_s", [NTC, 128, D], F32)]:
        d[name] = nc.dram_tensor(name, shape, dt, kind="Internal").ap()
    with ExitStack() as stack:
        block = stack.enter_context(nc.Block())
        E = Emit(nc, stack, block)
        E.d = d
        ph = E.c("ph")
        emit_setup(E, "C")
        emit_setup2(E, "C")
        gpf = _setup_f32_ident(E)
        emit_xnf_load(E, d["xn1_s"])
        scalar_fence(E, [(E.setup_done, E.setup_done.n), E.setup2_done,
                         (gpf, gpf.n), E.xnf_done], ph)
        emit_s1route(E, d["S1_in"])
        emit_scan2_y(E)
        scalar_fence(E, [E.s2_done], ph)
        emit_ffn_gu(E, "l1")
        scalar_fence(E, [E.gu_done], ph)
        emit_ffn_down(E, "l1")

        @E.block.gpsimd
        def _(g):
            ctr, n = E.down_done
            ctr.wait(g, n)
    return nc



def _prep_common(inputs):
    c = {}
    c["wqkv_t"] = np.ascontiguousarray(np.asarray(inputs["L0_attn_w_qkv"]).T)
    c["slope_c"] = np.ascontiguousarray(
        np.broadcast_to(np.asarray(inputs["L0_attn_slope_rate"])[None, :], (128, H)))
    c["wgate_t"] = np.ascontiguousarray(np.asarray(inputs["L0_attn_w_gate"]).T)
    c["wout_t"] = np.ascontiguousarray(np.asarray(inputs["L0_attn_w_out"]).T)
    c["normw_b"] = np.ascontiguousarray(
        np.broadcast_to(np.asarray(inputs["L0_attn_norm_w"])[None, :], (128, D)))
    c["w0g_t"] = np.ascontiguousarray(np.asarray(inputs["L0_ffn_w_gate"]).T)
    c["w0u_t"] = np.ascontiguousarray(np.asarray(inputs["L0_ffn_w_up"]).T)
    c["w0d_t"] = np.ascontiguousarray(np.asarray(inputs["L0_ffn_w_down"]).T)
    c["igw"] = np.ascontiguousarray(np.asarray(inputs["L1_attn_ig_w"], np.float32))
    c["agw"] = np.ascontiguousarray(np.asarray(inputs["L1_attn_ag_w"], np.float32))
    c["igb_c"] = np.ascontiguousarray(np.asarray(inputs["L1_attn_ig_b"], np.float32).T)
    c["agb_c"] = np.ascontiguousarray(np.asarray(inputs["L1_attn_ag_b"], np.float32).T)
    c["apar_c"] = np.ascontiguousarray(
        np.asarray(inputs["L1_attn_a_param"], np.float32).reshape(H, 128).T)
    c["w1g"] = np.ascontiguousarray(np.asarray(inputs["L1_ffn_w_gate"]))
    c["w1u"] = np.ascontiguousarray(np.asarray(inputs["L1_ffn_w_up"]))
    c["w1d"] = np.ascontiguousarray(np.asarray(inputs["L1_ffn_w_down"]))
    c["s1g_c"] = np.ascontiguousarray(
        np.asarray(inputs["L1_ffn_s_gate"], np.float32).reshape(64, 128).T)
    c["s1u_c"] = np.ascontiguousarray(
        np.asarray(inputs["L1_ffn_s_up"], np.float32).reshape(64, 128).T)
    c["s1d_b"] = np.ascontiguousarray(
        np.broadcast_to(np.asarray(inputs["L1_ffn_s_down"], np.float32)[None, :], (128, D)))
    return c


def _core_inputs_B(inputs, common, core, attn_s, q_s, S_in):
    b, half = core // 2, core % 2
    x = np.ascontiguousarray(
        np.asarray(inputs["L0_attn_hidden_states"])[b, half * TOK:(half + 1) * TOK])
    selp = np.full((128, 1), 1.0 if half == 1 else 0.0, np.float32)
    rcol = np.full((128, 1), 1.0 if half == 0 else 0.0, np.float32)
    omrcol = np.full((128, 1), 0.0 if half == 0 else 1.0, np.float32)
    m = dict(x=x, selp=selp, rcol=rcol, omrcol=omrcol,
             attn_s=attn_s, q_s=q_s, S_in=S_in)
    for k in ("slope_c", "wgate_t", "wout_t", "normw_b", "w0g_t", "w0u_t",
              "w0d_t", "igw", "agw", "igb_c", "agb_c", "apar_c"):
        m[k] = common[k]
    return m


def _core_inputs_C(inputs, common, core, x1_s, S1_in, a_s, nx_s, xn1_s):
    half = core % 2
    selp = np.full((128, 1), 1.0 if half == 1 else 0.0, np.float32)
    m = dict(selp=selp, x1_s=x1_s, S1_in=S1_in, a_s=a_s, nx_s=nx_s, xn1_s=xn1_s)
    for k in ("slope_c", "w1g", "w1u", "w1d", "s1g_c", "s1u_c", "s1d_b"):
        m[k] = common[k]
    return m


B_, SEQ = 4, 4096


FUSED = True


def kernel(**inputs):
    if FUSED:
        return kernel_fused(inputs)
    return kernel_unfused(inputs)


def kernel_unfused(inputs):
    common = _prep_common(inputs)
    ncA = build_A()
    in_A = [_core_inputs_A(inputs, c) for c in range(8)]
    resA = run_bass_kernel_spmd(ncA, in_A, core_ids=list(range(8))).results
    zS = np.zeros((D, DH), np.float32)
    S_ins = [resA[c - 1]["S_out"] if c % 2 == 1 else zS for c in range(8)]
    ncB = build_B()
    in_B = [_core_inputs_B(inputs, common, c, resA[c]["attn_s"], resA[c]["q_s"],
                           S_ins[c]) for c in range(8)]
    resB = run_bass_kernel_spmd(ncB, in_B, core_ids=list(range(8))).results
    zS1 = np.zeros((128, H), np.float32)
    S1_ins = [resB[c - 1]["S1_out"] if c % 2 == 1 else zS1 for c in range(8)]
    ncC = build_C()
    in_C = [_core_inputs_C(inputs, common, c, resB[c]["x1_s"], S1_ins[c],
                           resB[c]["a_s"], resB[c]["nx_s"], resB[c]["xn1_s"])
            for c in range(8)]
    resC = run_bass_kernel_spmd(ncC, in_C, core_ids=list(range(8))).results
    out = np.zeros((B_, SEQ, D), np.float32)
    for c in range(8):
        b, half = c // 2, c % 2
        out[b, half * TOK:(half + 1) * TOK] = resC[c]["out"]
    return out



PAIRS = [[0, 1], [2, 3], [4, 5], [6, 7]]


def build_FULL():
    nc = bass.Bass("TRN2", target_bir_lowering=False, debug=False,
                   detect_race_conditions=False, num_devices=8)
    d = {}
    for name, shape, dt in [
            ("x", [TOK, D], F32), ("wqkv_t", [D, 3 * D], F32),
            ("slope_c", [128, H], F32), ("selp", [128, 1], F32),
            ("wgate_t", [D, D], F32), ("wout_t", [D, D], F32), ("normw_b", [128, D], F32),
            ("w0g_t", [D, FFN], F32), ("w0u_t", [D, FFN], F32), ("w0d_t", [FFN, D], F32),
            ("igw", [H, 128, 128], F32), ("agw", [H, 128, 128], F32),
            ("igb_c", [128, H], F32), ("agb_c", [128, H], F32), ("apar_c", [128, H], F32),
            ("rcol", [128, 1], F32), ("omrcol", [128, 1], F32),
            ("w1g", [D, FFN], I8), ("w1u", [D, FFN], I8), ("w1d", [FFN, D], I8),
            ("s1g_c", [128, 64], F32), ("s1u_c", [128, 64], F32), ("s1d_b", [128, D], F32)]:
        d[name] = nc.dram_tensor(name, shape, dt, kind="ExternalInput").ap()
    d["out"] = nc.dram_tensor("out", [TOK, D], F32, kind="ExternalOutput").ap()
    for name, shape, dt in [
            ("attn_s", [TOK, D], F32), ("q_s", [H, NTC, 128, 128], BF16),
            ("x1p_s", [TOK, D], F32), ("x1_s", [TOK, D], F32),
            ("g_s", [4, FFN, 512], BF16), ("z_s", [NTC, D, 128], BF16),
            ("y_s", [NTC, 128, D], F32),
            ("a_s", [H, 128, TOK], F32), ("nx_s", [H, 128, TOK], F32),
            ("S_cc_in", [D, DH], F32), ("S_cc_out", [2 * D, DH], F32),
            ("S1_cc_in", [128, H], F32), ("S1_cc_out", [256, H], F32)]:
        d[name] = nc.dram_tensor(name, shape, dt, kind="Internal").ap()
    d["S_out"] = d["S_cc_in"]
    d["S1_out"] = d["S1_cc_in"]
    with ExitStack() as stack:
        block = stack.enter_context(nc.Block())
        E = Emit(nc, stack, block)
        E.d = d
        ph = E.c("ph")
        E.t["xnf"] = stack.enter_context(nc.sbuf_tensor("xnf", [128, NFC * TOK], BF16))
        emit_setup(E, "FULL")
        emit_setup2(E, "FULL")
        gpf = _setup_f32_ident(E)
        scalar_fence(E, [(E.setup_done, E.setup_done.n), E.setup2_done,
                         E.setup2g_done, (gpf, gpf.n)], ph)
        with ExitStack() as ascope:
            emit_attn_consts(E, ascope.enter_context)
            scalar_fence(E, [E.aconst_done], ph)
            emit_ln(E, d["x"], "l0")
            scalar_fence(E, [E.ln_done], ph)
            emit_attn_local(E)
            cc1 = E.c("cc1")

            @E.block.gpsimd
            def _(g):
                ctr, n = E.attn_done
                ctr.wait(g, n)
                cc1.inc(g.collective_compute(
                    "AllGather", ALU.bypass, replica_groups=PAIRS,
                    ins=[d["S_cc_in"]], outs=[d["S_cc_out"]]))
            scalar_fence(E, [E.attn_done], ph)
        with ExitStack() as cscope:
            emit_sroute(E, d["S_cc_out"][0:D, :], cc_wait=(cc1, cc1.n),
                        alloc=cscope.enter_context)
            scalar_fence(E, [E.sroute_done], ph)
            emit_corr_rms_gate_z(E)
            scalar_fence(E, [E.crz_done], ph)
        emit_ffn_gu(E, "l0")
        scalar_fence(E, [E.gu_done], ph)
        emit_ffn_down(E, "l0")
        scalar_fence(E, [E.down_done], ph)
        emit_outproj(E)
        scalar_fence(E, [E.op_done], ph)
        emit_ln(E, d["x1_s"], "l1")
        scalar_fence(E, [E.ln_done], ph)
        emit_l1_gates_scan(E)
        cc2 = E.c("cc2")

        @E.block.gpsimd
        def _(g):
            ctr, n = E.g1_done
            ctr.wait(g, n)
            cc2.inc(g.collective_compute(
                "AllGather", ALU.bypass, replica_groups=PAIRS,
                ins=[d["S1_cc_in"]], outs=[d["S1_cc_out"]]))
        scalar_fence(E, [E.g1_done], ph)
        emit_s1route(E, d["S1_cc_out"][0:128, :], cc_wait=(cc2, cc2.n))
        emit_scan2_y(E)
        scalar_fence(E, [E.s2_done], ph)
        emit_ffn_gu(E, "l1")
        scalar_fence(E, [E.gu_done], ph)
        emit_ffn_down(E, "l1")

        @E.block.gpsimd
        def _(g):
            ctr, n = E.down_done
            ctr.wait(g, n)
    return nc


def _core_inputs_FULL(inputs, common, core):
    b, half = core // 2, core % 2
    x = np.ascontiguousarray(
        np.asarray(inputs["L0_attn_hidden_states"])[b, half * TOK:(half + 1) * TOK])
    selp = np.full((128, 1), 1.0 if half == 1 else 0.0, np.float32)
    rcol = np.full((128, 1), 1.0 if half == 0 else 0.0, np.float32)
    omrcol = np.full((128, 1), 0.0 if half == 0 else 1.0, np.float32)
    m = dict(x=x, selp=selp, rcol=rcol, omrcol=omrcol)
    for k in ("wqkv_t", "slope_c", "wgate_t", "wout_t", "normw_b", "w0g_t", "w0u_t",
              "w0d_t", "igw", "agw", "igb_c", "agb_c", "apar_c",
              "w1g", "w1u", "w1d", "s1g_c", "s1u_c", "s1d_b"):
        m[k] = common[k]
    return m


def kernel_fused(inputs):
    common = _prep_common(inputs)
    ncF = build_FULL()
    in_F = [_core_inputs_FULL(inputs, common, c) for c in range(8)]
    res = run_bass_kernel_spmd(ncF, in_F, core_ids=list(range(8))).results
    out = np.zeros((B_, SEQ, D), np.float32)
    for c in range(8):
        b, half = c // 2, c % 2
        out[b, half * TOK:(half + 1) * TOK] = res[c]["out"]
    return out


# version 05
# speedup vs baseline: 1.1231x; 1.1231x over previous
import numpy as np
from contextlib import ExitStack

import concourse.bass as bass
import concourse.mybir as mybir
from concourse.bass_utils import run_bass_kernel_spmd

F32 = mybir.dt.float32
BF16 = mybir.dt.bfloat16
I8 = mybir.dt.int8
AF = mybir.ActivationFunctionType
ALU = mybir.AluOpType

TOK = 2048
D = 2048
H = 16
DH = 128
BLK = 256
NBLK = TOK // BLK
FFN = 8192
NTC = TOK // 128
NFC = D // 128


class Ctr:

    def __init__(self, stack, nc, name):
        self.sem = stack.enter_context(nc.semaphore(name))
        self.n = 0

    def inc(self, ins, k=1):
        ins.then_inc(self.sem, k)
        self.n += k
        return self.n

    def wait(self, eng, at=None):
        eng.wait_ge(self.sem, self.n if at is None else at)


class Emit:
    def __init__(self, nc, stack, block):
        self.nc = nc
        self.stack = stack
        self.block = block
        self.ctr = {}
        self.t = {}

    def c(self, name):
        if name not in self.ctr:
            self.ctr[name] = Ctr(self.stack, self.nc, name)
        return self.ctr[name]



def emit_setup(E, seg):
    nc = E.nc
    st = E.stack.enter_context
    E.t["ident"] = st(nc.sbuf_tensor("ident", [128, 128], BF16))
    E.t["slope_c"] = st(nc.sbuf_tensor("sb_slope_c", [128, 16], F32))
    E.t["negslope"] = st(nc.sbuf_tensor("negslope", [128, 16], F32))
    E.t["bd_cols"] = st(nc.sbuf_tensor("bd_cols", [128, 16], F32))
    E.t["selp"] = st(nc.sbuf_tensor("sb_selp", [128, 1], F32))
    E.t["fence_s"] = st(nc.sbuf_tensor("fence_s", [128, 128], F32))
    E.t["cc"] = st(nc.sbuf_tensor("cc", [128, 4], F32))
    if seg in ("B", "FULL"):
        E.t["qdecg_iota"] = st(nc.sbuf_tensor("qdecg_iota", [128, 16], F32))
        E.t["qdecg_t"] = st(nc.sbuf_tensor("qdecg_t", [128, H * 16], F32))

    ld = E.c("ld_const")
    gp = E.c("gp_setup")
    vs = E.c("v_setup")
    E.setup_done = E.c("s_setup")

    @E.block.sync
    def _(sync):
        ld.inc(sync.dma_start(out=E.t["slope_c"][:], in_=E.d["slope_c"]), 16)
        ld.inc(sync.dma_start(out=E.t["selp"][:], in_=E.d["selp"]), 16)

    @E.block.gpsimd
    def _(g):
        g.memset(E.t["ident"][:], 0)
        g.affine_select(out=E.t["ident"][:], in_=E.t["ident"][:],
                        compare_op=ALU.not_equal, fill=1.0, base=0,
                        pattern=[[-1, 128]], channel_multiplier=1)
        g.memset(E.t["cc"][:, 0:1], 1e-5)
        g.memset(E.t["cc"][:, 1:2], 1e-6)
        g.memset(E.t["cc"][:, 2:3], 1.0)
        ins = g.memset(E.t["cc"][:, 3:4], 0.0)
        if seg in ("B", "FULL"):
            ins = g.iota(E.t["qdecg_iota"][:], [[128, 16]], base=1, channel_multiplier=1,
                         allow_small_or_imprecise_dtypes=True)
        gp.inc(ins)

    @E.block.vector
    def _(v):
        ld.wait(v)
        E.c("v_setup").inc(v.tensor_scalar_mul(
            out=E.t["negslope"][:], in0=E.t["slope_c"][:], scalar1=-1.0))

    @E.block.scalar
    def _(s):
        ld.wait(s)
        gp.wait(s)
        vs.wait(s)
        s.activation(out=E.t["bd_cols"][:], in_=E.t["slope_c"][:], func=AF.Exp,
                     scale=-float(BLK))
        for h in range(H):
            nsl = E.t["negslope"][:, h:h + 1]
            if seg in ("B", "FULL"):
                for tc in range(16):
                    s.activation(out=E.t["qdecg_t"][:, h * 16 + tc:h * 16 + tc + 1],
                                 in_=E.t["qdecg_iota"][:, tc:tc + 1], func=AF.Exp, scale=nsl)
        E.setup_done.inc(s.copy(out=E.t["fence_s"][:, 0:1], in_=E.t["bd_cols"][:, 0:1]))


def emit_attn_consts(E, e):
    nc = E.nc
    E.t["idx0"] = e(nc.sbuf_tensor("idx0", [128, 256], F32))
    E.t["idx1"] = e(nc.sbuf_tensor("idx1", [128, 256], F32))
    E.t["qdec_iota"] = e(nc.sbuf_tensor("qdec_iota", [128, 2], F32))
    E.t["kdec_iota"] = e(nc.sbuf_tensor("kdec_iota", [128, 2], F32))
    E.t["Dtab"] = e(nc.sbuf_tensor("Dtab", [128, H * 2 * 256], BF16))
    E.t["qdec_t"] = e(nc.sbuf_tensor("qdec_t", [128, H * 2], F32))
    E.t["kdec_t"] = e(nc.sbuf_tensor("kdec_t", [128, H * 2], F32))
    gp2 = E.c("gp_aconst")
    sc2 = E.c("sc_aconst")

    @E.block.gpsimd
    def _(g):
        for kc, t in ((0, "idx0"), (1, "idx1")):
            g.iota(E.t[t][:], [[1, 256]], base=-128 * kc, channel_multiplier=-1,
                   allow_small_or_imprecise_dtypes=True)
            g.affine_select(out=E.t[t][:], in_=E.t[t][:],
                            compare_op=ALU.is_ge, fill=1e30, base=-128 * kc,
                            pattern=[[1, 256]], channel_multiplier=-1)
        g.iota(E.t["qdec_iota"][:], [[128, 2]], base=1, channel_multiplier=1,
               allow_small_or_imprecise_dtypes=True)
        gp2.inc(g.iota(E.t["kdec_iota"][:], [[-128, 2]], base=255, channel_multiplier=-1,
                       allow_small_or_imprecise_dtypes=True))

    @E.block.scalar
    def _(s):
        gp2.wait(s)
        E.c("v_setup").wait(s)
        Dt = E.t["Dtab"]
        ins = None
        for h in range(H):
            nsl = E.t["negslope"][:, h:h + 1]
            for kc in range(2):
                s.activation(out=Dt[:, (h * 2 + kc) * 256:(h * 2 + kc + 1) * 256],
                             in_=E.t["idx" + str(kc)][:], func=AF.Exp, scale=nsl)
                s.activation(out=E.t["qdec_t"][:, h * 2 + kc:h * 2 + kc + 1],
                             in_=E.t["qdec_iota"][:, kc:kc + 1], func=AF.Exp, scale=nsl)
                ins = s.activation(out=E.t["kdec_t"][:, h * 2 + kc:h * 2 + kc + 1],
                                   in_=E.t["kdec_iota"][:, kc:kc + 1], func=AF.Exp, scale=nsl)
        sc2.inc(ins)
    E.aconst_done = (sc2, sc2.n)


def scalar_fence(E, wait_list, ph):
    @E.block.scalar
    def _(s):
        for item in wait_list:
            if item is None:
                continue
            ctr, val = item
            ctr.wait(s, val)
        ph.inc(s.copy(out=E.t["fence_s"][:, 0:1], in_=E.t["fence_s"][:, 0:1]))



def emit_ln(E, src_dram, tag):
    nc = E.nc
    st = E.stack.enter_context
    if "xnf" not in E.t:
        E.t["xnf"] = st(nc.sbuf_tensor("xnf", [128, NFC * TOK], BF16))
    xnf = E.t["xnf"]
    phn = E.c("ph").n
    with ExitStack() as phs:
        e = phs.enter_context
        xtile = [e(nc.sbuf_tensor(f"ln_x{i}_{tag}", [128, D], F32)) for i in range(2)]
        xc = e(nc.sbuf_tensor(f"ln_xc_{tag}", [128, D], F32))
        sq = e(nc.sbuf_tensor(f"ln_sq_{tag}", [128, D], F32))
        xntm = [e(nc.sbuf_tensor(f"ln_xn{i}_{tag}", [128, D], BF16)) for i in range(2)]
        mcol = e(nc.sbuf_tensor(f"ln_m_{tag}", [128, 1], F32))
        m2 = e(nc.sbuf_tensor(f"ln_m2_{tag}", [128, 1], F32))
        ssq = e(nc.sbuf_tensor(f"ln_ssq_{tag}", [128, 1], F32))
        sd = e(nc.sbuf_tensor(f"ln_sd_{tag}", [128, 1], F32))
        rstd = e(nc.sbuf_tensor(f"ln_rstd_{tag}", [128, 1], F32))
        pst = [e(nc.psum_tensor(f"ln_ps{i}_{tag}", [128, 128], BF16)) for i in range(4)]

        ld = E.c("ld_ln")
        va = E.c("va_ln")
        vb = E.c("vb_ln")
        vf = E.c("vf_ln")
        sa = E.c("sa_ln")
        vc = E.c("vc_ln")
        tp = E.c("tp_ln")
        se = E.c("se_ln")
        ld0, vb0, vf0, sa0, vc0, tp0, se0 = ld.n, vb.n, vf.n, sa.n, vc.n, tp.n, se.n

        @E.block.sync
        def _(sync):
            for tc in range(NTC):
                if tc >= 2:
                    vf.wait(sync, vf0 + tc - 1)
                ld.inc(sync.dma_start(
                    out=xtile[tc % 2][:], in_=src_dram[tc * 128:(tc + 1) * 128, :]), 16)

        @E.block.vector
        def _(v):
            E.c("ph").wait(v, phn)
            for tc in range(NTC):
                ld.wait(v, ld0 + 16 * (tc + 1))
                va.inc(v.reduce_sum(mcol[:], xtile[tc % 2][:], axis=mybir.AxisListType.X))
                va.wait(v)
                va.inc(v.tensor_scalar_mul(out=m2[:], in0=mcol[:], scalar1=1.0 / D))
                va.wait(v)
                vb.inc(v.tensor_scalar(out=xc[:], in0=xtile[tc % 2][:],
                                       scalar1=m2[:], scalar2=None, op0=ALU.subtract))
                vf.inc(v.tensor_copy(out=mcol[:], in_=m2[:]))
                sa.wait(v, sa0 + 2 * (tc + 1))
                va.inc(v.reciprocal(rstd[:], sd[:]))
                va.wait(v)
                if tc >= 2:
                    tp.wait(v, tp0 + (tc - 1) * NFC)
                vc.inc(v.tensor_scalar_mul(out=xntm[tc % 2][:], in0=xc[:], scalar1=rstd[:]))

        @E.block.scalar
        def _(s):
            E.c("ph").wait(s, phn)

            def evac_tile(tc):
                for fc in range(NFC):
                    k = tc * NFC + fc
                    tp.wait(s, tp0 + k + 1)
                    se.inc(s.copy(out=xnf[:, fc * TOK + tc * 128: fc * TOK + (tc + 1) * 128],
                                  in_=pst[k % 4][:]))

            for tc in range(NTC):
                vb.wait(s, vb0 + tc + 1)
                sa.inc(s.activation(out=sq[:], in_=xc[:], func=AF.Square, accum_out=ssq[:]))
                sa.inc(s.activation(out=sd[:], in_=ssq[:], func=AF.Sqrt,
                                    scale=1.0 / D, bias=E.t["cc"][:, 0:1]))
                if tc >= 1:
                    evac_tile(tc - 1)
            evac_tile(NTC - 1)

        @E.block.tensor
        def _(t):
            E.c("ph").wait(t, phn)
            for tc in range(NTC):
                vc.wait(t, vc0 + tc + 1)
                for fc in range(NFC):
                    k = tc * NFC + fc
                    if k >= 4:
                        se.wait(t, se0 + k - 3)
                    tp.inc(t.transpose(pst[k % 4][:], xntm[tc % 2][:, fc * 128:(fc + 1) * 128],
                                       E.t["ident"][:]))

        E.ln_done = (se, se.n)



def emit_attn_local(E, depth="blocks"):
    nc = E.nc
    phn = E.c("ph").n
    with ExitStack() as phs:
        e = phs.enter_context
        wq = [e(nc.sbuf_tensor(f"wq{i}", [128, NFC * 128], BF16)) for i in range(2)]
        wk = [e(nc.sbuf_tensor(f"wk{i}", [128, NFC * 128], BF16)) for i in range(2)]
        wv = [e(nc.sbuf_tensor(f"wv{i}", [128, NFC * 128], BF16)) for i in range(2)]
        qf = [e(nc.sbuf_tensor(f"qf{i}", [128, TOK], BF16)) for i in range(2)]
        kf = [e(nc.sbuf_tensor(f"kf{i}", [128, TOK], BF16)) for i in range(2)]
        vf_ = [e(nc.sbuf_tensor(f"vf{i}", [128, TOK], BF16)) for i in range(2)]
        ktd = e(nc.sbuf_tensor("ktd", [128, TOK], BF16))
        vtm = e(nc.sbuf_tensor("vtm", [128, TOK], BF16))
        qkTd = e(nc.sbuf_tensor("qkTd", [128, 512], BF16))
        t1 = [e(nc.sbuf_tensor(f"t1_{i}", [128, 128], F32)) for i in range(2)]
        outsb = [e(nc.sbuf_tensor(f"attn_o{i}", [128, 128], F32)) for i in range(4)]
        kv = e(nc.sbuf_tensor("kv", [128, 128], F32))
        kvbf = e(nc.sbuf_tensor("kvbf", [128, 128], BF16))
        psA = e(nc.psum_tensor("psA", [128, 512], F32))
        psB = e(nc.psum_tensor("psB", [128, 512], F32))
        psT = [e(nc.psum_tensor(f"psT{i}", [128, 1024], BF16)) for i in range(2)]
        psQK = e(nc.psum_tensor("psQK", [128, 512], F32))
        psIE = [e(nc.psum_tensor(f"psIE{i}", [128, 512], F32)) for i in range(2)]
        psKV = e(nc.psum_tensor("psKV", [128, 512], F32))

        gq = E.c("gq_attn")
        mmq = E.c("mmq")
        evq = E.c("evq")
        tpa = E.c("tpa")
        eva = E.c("eva")
        kvb = E.c("kvb")
        qkm = E.c("qkm")
        qkd = E.c("qkd")
        iem = E.c("iem")
        t1r = E.c("t1r")
        oadd = E.c("oadd")
        kvm = E.c("kvm")
        kvu = E.c("kvu")
        msk = E.c("msk")
        spl = E.c("spl")
        (gq0, mmq0, evq0, tpa0, eva0, kvb0, qkm0, qkd0, iem0, t1r0, oadd0,
         kvm0, kvu0, msk0, spl0) = (gq.n, mmq.n, evq.n, tpa.n, eva.n, kvb.n,
                                    qkm.n, qkd.n, iem.n, t1r.n, oadd.n,
                                    kvm.n, kvu.n, msk.n, spl.n)
        KI = {"q": 0, "k": 1, "v": 2}

        def v_mmq(h, ki, tg):
            return mmq0 + h * 12 + ki * 4 + tg + 1

        def v_evq(h, ki, tg):
            return evq0 + h * 12 + ki * 4 + tg + 1

        def v_tpa(h, idx, tc):
            return tpa0 + h * 32 + idx * 16 + tc + 1

        def v_eva(h, idx, tc):
            return eva0 + h * 32 + idx * 16 + tc + 1

        per_head_spl = 33 if depth == "blocks" else 16

        def v_spl_q(h, tc):
            return spl0 + 16 * (h * per_head_spl + tc + 1)

        def v_spl_attn(h, l):
            return spl0 + 16 * (h * per_head_spl + 16 + l + 1)

        def v_spl_S(h):
            return spl0 + 16 * (h * per_head_spl + 33)

        @E.block.gpsimd
        def _(g):
            E.c("ph").wait(g, phn)
            for h in range(H):
                hb = h % 2
                if h >= 2:
                    mmq.wait(g, v_mmq(h - 2, 2, 3))
                for w_sb, col0 in ((wq, h * 128), (wk, D + h * 128), (wv, 2 * D + h * 128)):
                    gq.inc(g.dma_start(
                        out=w_sb[hb][:].rearrange("p (c o) -> p c o", o=128),
                        in_=E.d["wqkv_t"][:, col0:col0 + 128]
                        .rearrange("(c p) o -> p c o", p=128)), 16)

        @E.block.tensor
        def _(t):
            E.c("ph").wait(t, phn)
            for h in range(H):
                hb = h % 2
                gq.wait(t, gq0 + (h + 1) * 48)
                for kind, w_sb, dst_ps in (("q", wq, psA), ("k", wk, psB), ("v", wv, psA)):
                    ki = KI[kind]
                    for tg in range(4):
                        if tg > 0:
                            prev = v_evq(h, ki, tg - 1)
                        elif kind == "q":
                            prev = v_evq(h - 1, 2, 3) if h > 0 else None
                        elif kind == "k":
                            prev = v_evq(h - 1, 1, 3) if h > 0 else None
                        else:
                            prev = v_evq(h, 0, 3)
                        if prev is not None:
                            evq.wait(t, prev)
                        for fc in range(NFC):
                            ins = t.matmul(
                                dst_ps[:], w_sb[hb][:, fc * 128:(fc + 1) * 128],
                                E.t["xnf"][:, fc * TOK + tg * 512: fc * TOK + (tg + 1) * 512],
                                start=(fc == 0), stop=(fc == NFC - 1))
                        mmq.inc(ins)
                for idx, src_sb in (((0, kf), (1, vf_)) if depth != "qkv" else ()):
                    ki = 1 + idx
                    for tc in range(NTC):
                        k = idx * 16 + tc
                        evq.wait(t, v_evq(h, ki, tc // 4))
                        gk = h * 32 + k
                        if gk >= 2:
                            eva.wait(t, eva0 + gk - 1)
                        tpa.inc(t.transpose(psT[k % 2][:, 0:128],
                                            src_sb[hb][:, tc * 128:(tc + 1) * 128],
                                            E.t["ident"][:]))
                if depth != "blocks":
                    continue
                evq.wait(t, v_evq(h, 2, 3))
                eva.wait(t, v_eva(h, 1, 15))
                for i in range(NBLK):
                    kvb.wait(t, kvb0 + h * 8 + i + 1)
                    if i > 0 or h > 0:
                        qkd.wait(t, qkd0 + h * 8 + i)
                    for kc in range(2):
                        ins = t.matmul(psQK[:, kc * 256:(kc + 1) * 256],
                                       kf[hb][:, (2 * i + kc) * 128:(2 * i + kc + 1) * 128],
                                       qf[hb][:, i * 256:(i + 1) * 256],
                                       start=True, stop=True)
                    qkm.inc(ins)
                    qkd.wait(t, qkd0 + h * 8 + i + 1)
                    for mc in range(2):
                        s_glob = h * 16 + i * 2 + mc
                        pse = psIE[s_glob % 2]
                        if s_glob >= 2:
                            oadd.wait(t, oadd0 + s_glob - 1)
                        for kc in range(2):
                            t.matmul(pse[:, 0:128],
                                     qkTd[:, kc * 256 + mc * 128: kc * 256 + (mc + 1) * 128],
                                     vtm[:, (2 * i + kc) * 128:(2 * i + kc + 1) * 128],
                                     start=(kc == 0), stop=(kc == 1))
                        ins = t.matmul(pse[:, 128:256],
                                       qf[hb][:, (i * 2 + mc) * 128:(i * 2 + mc + 1) * 128],
                                       kvbf[:], start=True, stop=True)
                        iem.inc(ins)
                    if i > 0 or h > 0:
                        kvu.wait(t, kvu0 + h * 8 + i)
                    for kc in range(2):
                        ins = t.matmul(psKV[:, 0:128],
                                       ktd[:, (2 * i + kc) * 128:(2 * i + kc + 1) * 128],
                                       vtm[:, (2 * i + kc) * 128:(2 * i + kc + 1) * 128],
                                       start=(kc == 0), stop=(kc == 1))
                    kvm.inc(ins)

        @E.block.scalar
        def _(s):
            E.c("ph").wait(s, phn)
            for h in range(H):
                hb = h % 2
                for kind, dst_ps, dst_sb in (("q", psA, qf), ("k", psB, kf), ("v", psA, vf_)):
                    ki = KI[kind]
                    if kind == "q" and h >= 2:
                        spl.wait(s, v_spl_q(h - 2, 15))
                    for tg in range(4):
                        mmq.wait(s, v_mmq(h, ki, tg))
                        evq.inc(s.activation(out=dst_sb[hb][:, tg * 512:(tg + 1) * 512],
                                             in_=dst_ps[:], func=AF.Silu))
                for i in range(NBLK if depth == "blocks" else 0):
                    for mc in range(2):
                        sl = h * 16 + i * 2 + mc
                        iem.wait(s, iem0 + sl + 1)
                        if sl >= 2:
                            oadd.wait(s, oadd0 + sl - 1)
                        t1r.inc(s.activation(
                            out=t1[sl % 2][:], in_=psIE[sl % 2][:, 128:256],
                            func=AF.Copy,
                            scale=E.t["qdec_t"][:, h * 2 + mc:h * 2 + mc + 1]))

        @E.block.vector
        def _(v):
            E.c("ph").wait(v, phn)
            for h in range(H):
                hb = h % 2
                if depth == "blocks":
                    if h >= 1:
                        spl.wait(v, v_spl_S(h - 1))
                    msk.inc(v.memset(kv[:], 0.0))
                for idx in range(2 if depth != "qkv" else 0):
                    for tc in range(NTC):
                        k = idx * 16 + tc
                        tpa.wait(v, v_tpa(h, idx, tc))
                        if idx == 0:
                            cp = tc % 2
                            ins = v.tensor_scalar_mul(
                                out=ktd[:, tc * 128:(tc + 1) * 128],
                                in0=psT[k % 2][:, 0:128],
                                scalar1=E.t["kdec_t"][:, h * 2 + cp:h * 2 + cp + 1])
                        else:
                            ins = v.tensor_copy(out=vtm[:, tc * 128:(tc + 1) * 128],
                                                in_=psT[k % 2][:, 0:128])
                        eva.inc(ins)
                for i in range(NBLK if depth == "blocks" else 0):
                    if i == 0:
                        msk.wait(v, msk0 + h + 1)
                    kvb.inc(v.tensor_copy(out=kvbf[:], in_=kv[:]))
                    qkm.wait(v, qkm0 + h * 8 + i + 1)
                    Dt = E.t["Dtab"]
                    for kc in range(2):
                        ins = v.tensor_tensor(
                            out=qkTd[:, kc * 256:(kc + 1) * 256],
                            in0=psQK[:, kc * 256:(kc + 1) * 256],
                            in1=Dt[:, (h * 2 + kc) * 256:(h * 2 + kc + 1) * 256],
                            op=ALU.mult)
                    qkd.inc(ins)
                    for mc in range(2):
                        sl = h * 16 + i * 2 + mc
                        t1r.wait(v, t1r0 + sl + 1)
                        if sl >= 4:
                            h2, l2 = (sl - 4) // 16, (sl - 4) % 16
                            spl.wait(v, v_spl_attn(h2, l2))
                        oadd.inc(v.tensor_tensor(out=outsb[sl % 4][:], in0=t1[sl % 2][:],
                                                 in1=psIE[sl % 2][:, 0:128],
                                                 op=ALU.add))
                    kvm.wait(v, kvm0 + h * 8 + i + 1)
                    kvu.inc(v.scalar_tensor_tensor(out=kv[:], in0=kv[:],
                                                   scalar=E.t["bd_cols"][:, h:h + 1],
                                                   in1=psKV[:, 0:128], op0=ALU.mult, op1=ALU.add))

        @E.block.sync
        def _(sync):
            for h in range(H):
                hb = h % 2
                evq.wait(sync, v_evq(h, 0, 3))
                for tc in range(NTC):
                    spl.inc(sync.dma_start(
                        out=E.d["q_s"][h, tc], in_=qf[hb][:, tc * 128:(tc + 1) * 128]), 16)
                for i in range(NBLK if depth == "blocks" else 0):
                    for mc in range(2):
                        sl = h * 16 + i * 2 + mc
                        oadd.wait(sync, oadd0 + sl + 1)
                        r0 = i * 256 + mc * 128
                        spl.inc(sync.dma_start(
                            out=E.d["attn_s"][r0:r0 + 128, h * 128:(h + 1) * 128],
                            in_=outsb[sl % 4][:]), 16)
                if depth == "blocks":
                    kvu.wait(sync, kvu0 + h * 8 + 8)
                    spl.inc(sync.dma_start(
                        out=E.d["S_out"][h * 128:(h + 1) * 128, :], in_=kv[:]), 16)

        E.attn_done = (spl, spl.n)



def _core_inputs_A(inputs, core):
    b, half = core // 2, core % 2
    x = np.ascontiguousarray(
        np.asarray(inputs["L0_attn_hidden_states"])[b, half * TOK:(half + 1) * TOK])
    wqkv_t = np.ascontiguousarray(np.asarray(inputs["L0_attn_w_qkv"]).T)
    slope_c = np.ascontiguousarray(
        np.broadcast_to(np.asarray(inputs["L0_attn_slope_rate"])[None, :], (128, H)))
    selp = np.full((128, 1), 1.0 if half == 1 else 0.0, np.float32)
    return dict(x=x, wqkv_t=wqkv_t, slope_c=slope_c, selp=selp)


def build_A(upto="attn"):
    nc = bass.Bass("TRN2", target_bir_lowering=False, debug=False, detect_race_conditions=False)
    d = {}
    d["x"] = nc.dram_tensor("x", [TOK, D], F32, kind="ExternalInput").ap()
    d["wqkv_t"] = nc.dram_tensor("wqkv_t", [D, 3 * D], F32, kind="ExternalInput").ap()
    d["slope_c"] = nc.dram_tensor("slope_c", [128, H], F32, kind="ExternalInput").ap()
    d["selp"] = nc.dram_tensor("selp", [128, 1], F32, kind="ExternalInput").ap()
    d["attn_s"] = nc.dram_tensor("attn_s", [TOK, D], F32, kind="ExternalOutput").ap()
    d["q_s"] = nc.dram_tensor("q_s", [H, NTC, 128, 128], BF16, kind="ExternalOutput").ap()
    d["S_out"] = nc.dram_tensor("S_out", [D, DH], F32, kind="ExternalOutput").ap()
    with ExitStack() as stack:
        block = stack.enter_context(nc.Block())
        E = Emit(nc, stack, block)
        E.d = d
        ph = E.c("ph")
        emit_setup(E, "A")
        emit_attn_consts(E, stack.enter_context)
        scalar_fence(E, [(E.setup_done, E.setup_done.n), E.aconst_done], ph)
        if upto != "setup":
            emit_ln(E, d["x"], "l0")
            scalar_fence(E, [E.ln_done], ph)
        if upto.startswith("attn"):
            emit_attn_local(E, depth=upto[5:] or "blocks")
        if upto != "attn":
            dn = E.c("drain")

            @E.block.sync
            def _(sync):
                E.c("ph").wait(sync)
                dn.inc(sync.dma_start(out=d["S_out"][0:1, 0:128], in_=E.t["fence_s"][0:1, :]), 16)

            @E.block.gpsimd
            def _(g):
                if upto.startswith("attn"):
                    ctr, n = E.attn_done
                    ctr.wait(g, n)
                dn.wait(g)
        else:
            @E.block.gpsimd
            def _(g):
                ctr, n = E.attn_done
                ctr.wait(g, n)
    return nc



def emit_setup2(E, seg):
    nc = E.nc
    st = E.stack.enter_context
    ld = E.c("ld_const2")
    need_b = seg in ("B", "FULL")
    need_c = seg in ("C", "FULL")

    if need_b:
        E.t["normw_b"] = st(nc.sbuf_tensor("sb_normw_b", [128, D], F32))
        E.t["igw"] = st(nc.sbuf_tensor("sb_igw", [128, H * 128], BF16))
        E.t["agw"] = st(nc.sbuf_tensor("sb_agw", [128, H * 128], BF16))
        E.t["igb"] = st(nc.sbuf_tensor("sb_igb", [128, H], F32))
        E.t["agb"] = st(nc.sbuf_tensor("sb_agb", [128, H], F32))
        E.t["apar"] = st(nc.sbuf_tensor("sb_apar", [128, H], F32))
        E.t["m8spa"] = st(nc.sbuf_tensor("sb_m8spa", [128, H], F32))
        E.t["rcol"] = st(nc.sbuf_tensor("sb_rcol", [128, 1], F32))
        E.t["omrcol"] = st(nc.sbuf_tensor("sb_omrcol", [128, 1], F32))
        E.t["spa_t1"] = st(nc.sbuf_tensor("sb_spa_t1", [128, H], F32))
        E.t["spa_t2"] = st(nc.sbuf_tensor("sb_spa_t2", [128, H], F32))
    if need_c:
        E.t["s1g"] = st(nc.sbuf_tensor("sb_s1g", [128, 64], F32))
        E.t["s1u"] = st(nc.sbuf_tensor("sb_s1u", [128, 64], F32))

    @E.block.sync
    def _(sync):
        if need_b:
            ld.inc(sync.dma_start(out=E.t["normw_b"][:], in_=E.d["normw_b"]), 16)
            ld.inc(sync.dma_start(out=E.t["igb"][:], in_=E.d["igb_c"]), 16)
            ld.inc(sync.dma_start(out=E.t["agb"][:], in_=E.d["agb_c"]), 16)
            ld.inc(sync.dma_start(out=E.t["apar"][:], in_=E.d["apar_c"]), 16)
            ld.inc(sync.dma_start(out=E.t["rcol"][:], in_=E.d["rcol"]), 16)
            ld.inc(sync.dma_start(out=E.t["omrcol"][:], in_=E.d["omrcol"]), 16)
        if need_c:
            ld.inc(sync.dma_start(out=E.t["s1g"][:], in_=E.d["s1g_c"]), 16)
            ld.inc(sync.dma_start(out=E.t["s1u"][:], in_=E.d["s1u_c"]), 16)

    if need_b:
        ldg = E.c("ld_const2g")

        @E.block.gpsimd
        def _(g):
            ldg.inc(g.dma_start(out=E.t["igw"][:].rearrange("i (h j) -> i h j", j=128),
                                in_=E.d["igw"].rearrange("h i j -> i h j")), 16)
            ldg.inc(g.dma_start(out=E.t["agw"][:].rearrange("i (h j) -> i h j", j=128),
                                in_=E.d["agw"].rearrange("h i j -> i h j")), 16)

        sp = E.c("sp_setup")

        @E.block.scalar
        def _(s):
            ld.wait(s)
            E.c("gp_setup").wait(s)
            s.activation(out=E.t["spa_t2"][:], in_=E.t["apar"][:], func=AF.Abs)
            s.activation(out=E.t["spa_t2"][:], in_=E.t["spa_t2"][:], func=AF.Exp, scale=-1.0)
            sp.inc(s.activation(out=E.t["spa_t2"][:], in_=E.t["spa_t2"][:], func=AF.Ln,
                                bias=E.t["cc"][:, 2:3]))

        @E.block.vector
        def _(v):
            ld.wait(v)
            sp.inc(v.tensor_scalar_max(out=E.t["spa_t1"][:], in0=E.t["apar"][:],
                                       scalar1=0.0))
            sp.wait(v, 2)
            sp.inc(v.tensor_tensor(out=E.t["m8spa"][:], in0=E.t["spa_t1"][:],
                                   in1=E.t["spa_t2"][:], op=ALU.add))
            sp.wait(v, 3)
            sp.inc(v.tensor_scalar_mul(out=E.t["m8spa"][:], in0=E.t["m8spa"][:],
                                       scalar1=-8.0))
        E.setup2_done = (sp, sp.n)
        E.setup2g_done = (ldg, ldg.n)
    else:
        E.setup2_done = (ld, ld.n)
        E.setup2g_done = None



def emit_sroute(E, src_ap, cc_wait=None, alloc=None):
    nc = E.nc
    st = alloc if alloc is not None else E.stack.enter_context
    E.t["Sbf"] = st(nc.sbuf_tensor("Sbf", [128, H * 128], BF16))
    phn = E.c("ph").n
    ld = E.c("ld_sroute")
    srt = E.c("srt")
    with ExitStack() as phs:
        Ssb = phs.enter_context(nc.sbuf_tensor("Ssb", [128, H * 128], F32))

        @E.block.sync
        def _(sync):
            E.c("ph").wait(sync, phn)
            if cc_wait is not None:
                cc_wait[0].wait(sync, cc_wait[1])
            ld.inc(sync.dma_start(
                out=Ssb[:].rearrange("p (h e) -> p h e", e=128),
                in_=src_ap.rearrange("(h p) e -> p h e", p=128)), 16)

        @E.block.vector
        def _(v):
            E.c("ph").wait(v, phn)
            ld.wait(v)
            v.tensor_scalar_mul(out=Ssb[:], in0=Ssb[:], scalar1=E.t["selp"][:])
            srt.inc(v.tensor_copy(out=E.t["Sbf"][:], in_=Ssb[:]))
    E.sroute_done = (srt, srt.n)



def emit_corr_rms_gate_z(E):
    nc = E.nc
    phn = E.c("ph").n
    with ExitStack() as phs:
        e = phs.enter_context
        wgt = e(nc.sbuf_tensor("wgt", [128, NFC * D], BF16))
        atile = [e(nc.sbuf_tensor(f"cr_at{i}", [128, D], F32)) for i in range(2)]
        qtl = e(nc.sbuf_tensor("cr_q", [128, H * 128], BF16))
        corr = e(nc.sbuf_tensor("cr_c", [128, D], F32))
        af = e(nc.sbuf_tensor("cr_af", [128, D], F32))
        ztm = [e(nc.sbuf_tensor(f"cr_z{i}", [128, D], BF16)) for i in range(2)]
        sg = [e(nc.sbuf_tensor(f"cr_sg{i}", [128, 512], F32)) for i in range(2)]
        zev = [e(nc.sbuf_tensor(f"cr_ze{i}", [128, 128], BF16)) for i in range(4)]
        ssq = e(nc.sbuf_tensor("cr_ssq", [128, 1], F32))
        sd = e(nc.sbuf_tensor("cr_sd", [128, 1], F32))
        rstd = e(nc.sbuf_tensor("cr_rstd", [128, 1], F32))
        psR = [e(nc.psum_tensor(f"cr_psR{i}", [128, 512], F32)) for i in range(2)]
        psG = [e(nc.psum_tensor(f"cr_psG{i}", [128, 512], F32)) for i in range(2)]
        psT = [e(nc.psum_tensor(f"cr_psT{i}", [128, 1024], BF16)) for i in range(2)]

        wld = E.c("cr_wld")
        ald = E.c("cr_ald")
        qld = E.c("cr_qld")
        crm = E.c("cr_crm")
        cre = E.c("cr_cre")
        vaf = E.c("cr_vaf")
        sa2 = E.c("cr_sa2")
        vt2 = E.c("cr_vt2")
        va = E.c("cr_va")
        gmm = E.c("cr_gmm")
        sgv = E.c("cr_sgv")
        vzt = E.c("cr_vzt")
        ztp = E.c("cr_ztp")
        zec = E.c("cr_zec")
        zspl = E.c("cr_zspl")
        (wld0, ald0, qld0, crm0, cre0, vaf0, sa20, vt20, gmm0, sgv0, vzt0,
         ztp0, zec0, zspl0) = (wld.n, ald.n, qld.n, crm.n, cre.n, vaf.n, sa2.n,
                               vt2.n, gmm.n, sgv.n, vzt.n, ztp.n, zec.n, zspl.n)

        @E.block.gpsimd
        def _(g):
            E.c("ph").wait(g, phn)
            for i in range(4):
                wld.inc(g.dma_start(
                    out=wgt[:, i * 4 * D:(i + 1) * 4 * D].rearrange("p (c o) -> p c o", o=D),
                    in_=E.d["wgate_t"][i * 512:(i + 1) * 512, :]
                    .rearrange("(c p) o -> p c o", p=128)), 16)
            for tc in range(NTC):
                if tc >= 1:
                    crm.wait(g, crm0 + tc * H)
                qld.inc(g.dma_start(
                    out=qtl[:].rearrange("p (h d) -> p h d", d=128),
                    in_=E.d["q_s"][:, tc].rearrange("h p d -> p h d")), 16)

        @E.block.sync
        def _(sync):
            E.c("ph").wait(sync, phn)

            def spill_tile(tc):
                for fc in range(NFC):
                    k = tc * NFC + fc
                    zec.wait(sync, zec0 + k + 1)
                    zspl.inc(sync.dma_start(
                        out=E.d["z_s"][tc, fc * 128:(fc + 1) * 128, :],
                        in_=zev[k % 4][:]), 16)

            for tc in range(NTC):
                if tc >= 2:
                    vaf.wait(sync, vaf0 + tc - 1)
                ald.inc(sync.dma_start(
                    out=atile[tc % 2][:],
                    in_=E.d["attn_s"][tc * 128:(tc + 1) * 128, :]), 16)
                if tc >= 1:
                    spill_tile(tc - 1)
            spill_tile(NTC - 1)

        @E.block.tensor
        def _(t):
            E.c("ph").wait(t, phn)
            ctr, n = E.sroute_done
            ctr.wait(t, n)
            for tc in range(NTC):
                qld.wait(t, qld0 + 16 * (tc + 1))
                for h in range(H):
                    r = tc * H + h
                    if r >= 2:
                        cre.wait(t, cre0 + r - 1)
                    crm.inc(t.matmul(psR[r % 2][:, 0:128],
                                     qtl[:, h * 128:(h + 1) * 128],
                                     E.t["Sbf"][:, h * 128:(h + 1) * 128],
                                     start=True, stop=True))
                if tc == 0:
                    wld.wait(t, wld0 + 64)
                for og in range(4):
                    z = tc * 4 + og
                    if z >= 2:
                        sgv.wait(t, sgv0 + z - 1)
                    for fc in range(NFC):
                        ins = t.matmul(
                            psG[z % 2][:],
                            E.t["xnf"][:, fc * TOK + tc * 128: fc * TOK + (tc + 1) * 128],
                            wgt[:, fc * D + og * 512: fc * D + (og + 1) * 512],
                            start=(fc == 0), stop=(fc == NFC - 1))
                    gmm.inc(ins)
                for fc in range(NFC):
                    k = tc * NFC + fc
                    vzt.wait(t, vzt0 + tc * 4 + fc // 4 + 1)
                    if k >= 2:
                        zec.wait(t, zec0 + k - 1)
                    ztp.inc(t.transpose(psT[k % 2][:, 0:128],
                                        ztm[tc % 2][:, fc * 128:(fc + 1) * 128],
                                        E.t["ident"][:]))

        @E.block.scalar
        def _(s):
            E.c("ph").wait(s, phn)
            for tc in range(NTC):
                for h in range(H):
                    r = tc * H + h
                    crm.wait(s, crm0 + r + 1)
                    if h == 0 and tc >= 1:
                        vaf.wait(s, vaf0 + tc)
                    cre.inc(s.activation(
                        out=corr[:, h * 128:(h + 1) * 128],
                        in_=psR[r % 2][:, 0:128], func=AF.Copy,
                        scale=E.t["qdecg_t"][:, h * 16 + tc:h * 16 + tc + 1]))
                vaf.wait(s, vaf0 + tc + 1)
                sa2.inc(s.activation(out=corr[:], in_=af[:], func=AF.Square,
                                     accum_out=ssq[:]))
                sa2.inc(s.activation(out=sd[:], in_=ssq[:], func=AF.Sqrt,
                                     scale=1.0 / D, bias=E.t["cc"][:, 1:2]))
                for og in range(4):
                    z = tc * 4 + og
                    gmm.wait(s, gmm0 + z + 1)
                    if z >= 2:
                        vzt.wait(s, vzt0 + z - 1)
                    sgv.inc(s.activation(out=sg[z % 2][:], in_=psG[z % 2][:],
                                         func=AF.Sigmoid))

        @E.block.vector
        def _(v):
            E.c("ph").wait(v, phn)
            for tc in range(NTC):
                ald.wait(v, ald0 + 16 * (tc + 1))
                cre.wait(v, cre0 + (tc + 1) * H)
                if tc >= 1:
                    sa2.wait(v, sa20 + 2 * tc - 1)
                vaf.inc(v.tensor_tensor(out=af[:], in0=atile[tc % 2][:],
                                        in1=corr[:], op=ALU.add))
                sa2.wait(v, sa20 + 2 * (tc + 1))
                va.inc(v.reciprocal(rstd[:], sd[:]))
                va.wait(v)
                vt2.inc(v.scalar_tensor_tensor(out=atile[tc % 2][:], in0=af[:],
                                               scalar=rstd[:], in1=E.t["normw_b"][:],
                                               op0=ALU.mult, op1=ALU.mult))
                for og in range(4):
                    z = tc * 4 + og
                    sgv.wait(v, sgv0 + z + 1)
                    if og == 0 and tc >= 2:
                        ztp.wait(v, ztp0 + (tc - 1) * NFC)
                    vzt.inc(v.tensor_tensor(
                        out=ztm[tc % 2][:, og * 512:(og + 1) * 512],
                        in0=atile[tc % 2][:, og * 512:(og + 1) * 512], in1=sg[z % 2][:],
                        op=ALU.mult))
                for fc in range(NFC):
                    k = tc * NFC + fc
                    ztp.wait(v, ztp0 + k + 1)
                    if k >= 4:
                        zspl.wait(v, zspl0 + 16 * (k - 3))
                    zec.inc(v.tensor_copy(out=zev[k % 4][:], in_=psT[k % 2][:, 0:128]))

        zspl.n = zspl.n
        E.crz_done = (zspl, zspl0 + 16 * NTC * NFC)
        zspl.n = zspl0 + 16 * NTC * NFC



def emit_ffn_gu(E, layer):
    nc = E.nc
    phn = E.c("ph").n
    NHC = FFN // 128
    with ExitStack() as phs:
        e = phs.enter_context
        wg = [e(nc.sbuf_tensor(f"fg_wg{i}_{layer}", [128, NFC * 128], BF16)) for i in range(2)]
        wu = [e(nc.sbuf_tensor(f"fg_wu{i}_{layer}", [128, NFC * 128], BF16)) for i in range(2)]
        sgt = [e(nc.sbuf_tensor(f"fg_sg{i}_{layer}", [128, 512], F32)) for i in range(2)]
        gtile = [e(nc.sbuf_tensor(f"fg_g{i}_{layer}", [128, 512], BF16)) for i in range(4)]
        psGg = [e(nc.psum_tensor(f"fg_psG{i}_{layer}", [128, 512], F32)) for i in range(2)]
        psU = [e(nc.psum_tensor(f"fg_psU{i}_{layer}", [128, 512], F32)) for i in range(2)]

        wld = E.c("fg_wld")
        mgu = E.c("fg_mgu")
        sgv = E.c("fg_sgv")
        vgg = E.c("fg_vgg")
        gspl = E.c("fg_gspl")
        wld0, mgu0, sgv0, vgg0, gspl0 = wld.n, mgu.n, sgv.n, vgg.n, gspl.n

        if layer == "l0":
            srcs = (E.d["w0g_t"], E.d["w0u_t"])
        else:
            srcs = (E.d["w1g"], E.d["w1u"])

        @E.block.gpsimd
        def _(g):
            E.c("ph").wait(g, phn)
            for hc in range(NHC):
                if hc >= 2:
                    mgu.wait(g, mgu0 + (hc - 1) * 8)
                for w_sb, src in ((wg, srcs[0]), (wu, srcs[1])):
                    wld.inc(g.dma_start(
                        out=w_sb[hc % 2][:].rearrange("p (c o) -> p c o", o=128),
                        in_=src[:, hc * 128:(hc + 1) * 128]
                        .rearrange("(c p) o -> p c o", p=128)), 16)

        @E.block.tensor
        def _(t):
            E.c("ph").wait(t, phn)
            for hc in range(NHC):
                wld.wait(t, wld0 + 32 * (hc + 1))
                for tg in range(4):
                    z = hc * 4 + tg
                    if z >= 2:
                        sgv.wait(t, sgv0 + z - 1)
                    for fc in range(NFC):
                        ins = t.matmul(
                            psGg[z % 2][:], wg[hc % 2][:, fc * 128:(fc + 1) * 128],
                            E.t["xnf"][:, fc * TOK + tg * 512: fc * TOK + (tg + 1) * 512],
                            start=(fc == 0), stop=(fc == NFC - 1))
                    mgu.inc(ins)
                    if z >= 2:
                        vgg.wait(t, vgg0 + z - 1)
                    for fc in range(NFC):
                        ins = t.matmul(
                            psU[z % 2][:], wu[hc % 2][:, fc * 128:(fc + 1) * 128],
                            E.t["xnf"][:, fc * TOK + tg * 512: fc * TOK + (tg + 1) * 512],
                            start=(fc == 0), stop=(fc == NFC - 1))
                    mgu.inc(ins)

        @E.block.scalar
        def _(s):
            E.c("ph").wait(s, phn)
            for hc in range(NHC):
                for tg in range(4):
                    z = hc * 4 + tg
                    mgu.wait(s, mgu0 + z * 2 + 1)
                    if z >= 2:
                        vgg.wait(s, vgg0 + z - 1)
                    if layer == "l0":
                        sgv.inc(s.activation(out=sgt[z % 2][:], in_=psGg[z % 2][:],
                                             func=AF.Silu))
                    else:
                        sgv.inc(s.activation(out=sgt[z % 2][:], in_=psGg[z % 2][:],
                                             func=AF.Silu,
                                             scale=E.t["s1g"][:, hc:hc + 1]))

        @E.block.vector
        def _(v):
            E.c("ph").wait(v, phn)
            for hc in range(NHC):
                for tg in range(4):
                    z = hc * 4 + tg
                    mgu.wait(v, mgu0 + z * 2 + 2)
                    sgv.wait(v, sgv0 + z + 1)
                    if z >= 4:
                        gspl.wait(v, gspl0 + 16 * (z - 3))
                    if layer == "l0":
                        vgg.inc(v.tensor_tensor(out=gtile[z % 4][:], in0=sgt[z % 2][:],
                                                in1=psU[z % 2][:], op=ALU.mult))
                    else:
                        vgg.inc(v.scalar_tensor_tensor(
                            out=gtile[z % 4][:], in0=psU[z % 2][:],
                            scalar=E.t["s1u"][:, hc:hc + 1], in1=sgt[z % 2][:],
                            op0=ALU.mult, op1=ALU.mult))

        @E.block.sync
        def _(sync):
            for hc in range(NHC):
                for tg in range(4):
                    z = hc * 4 + tg
                    vgg.wait(sync, vgg0 + z + 1)
                    gspl.inc(sync.dma_start(
                        out=E.d["g_s"][tg, hc * 128:(hc + 1) * 128, :],
                        in_=gtile[z % 4][:]), 16)
        E.gu_done = (gspl, gspl.n)



def emit_ffn_down(E, layer):
    nc = E.nc
    phn = E.c("ph").n
    NHC = FFN // 128
    with ExitStack() as phs:
        e = phs.enter_context
        wd = e(nc.sbuf_tensor(f"fdw_{layer}", [128, 64 * 512], BF16))
        gb = [e(nc.sbuf_tensor(f"fd_gb{i}_{layer}", [128, 64 * 128], BF16)) for i in range(2)]
        res1 = [e(nc.sbuf_tensor(f"fd_r1{i}_{layer}", [128, 512], F32)) for i in range(2)]
        osb = [e(nc.sbuf_tensor(f"fd_o{i}_{layer}", [128, 512], F32)) for i in range(4)]
        if layer == "l1":
            res2 = [e(nc.sbuf_tensor(f"fd_r2{i}_{layer}", [128, 512], F32)) for i in range(2)]
            tmp = e(nc.sbuf_tensor(f"fd_tmp_{layer}", [128, 512], F32))
            s1d = e(nc.sbuf_tensor("fd_s1d", [128, D], F32))
        psD = [e(nc.psum_tensor(f"fd_psD{i}_{layer}", [128, 512], F32)) for i in range(2)]

        wld = E.c("fd_wld")
        gld = E.c("fd_gld")
        gld2 = E.c("fd_gld2")
        rld = E.c("fd_rld")
        mdn = E.c("fd_mdn")
        vdn = E.c("fd_vdn")
        ospl = E.c("fd_ospl")
        (wld0, gld0, gld20, rld0, mdn0, vdn0, ospl0) = (
            wld.n, gld.n, gld2.n, rld.n, mdn.n, vdn.n, ospl.n)
        wsrc = E.d["w0d_t"] if layer == "l0" else E.d["w1d"]
        rper = 1 if layer == "l0" else 2

        @E.block.gpsimd
        def _(g):
            E.c("ph").wait(g, phn)
            for dblk in range(4):
                if dblk >= 1:
                    mdn.wait(g, mdn0 + dblk * 16)
                for grp in range(8):
                    wld.inc(g.dma_start(
                        out=wd[:, grp * 8 * 512:(grp + 1) * 8 * 512]
                        .rearrange("p (c o) -> p c o", o=512),
                        in_=wsrc[grp * 1024:(grp + 1) * 1024,
                                 dblk * 512:(dblk + 1) * 512]
                        .rearrange("(c p) o -> p c o", p=128)), 16)

        @E.block.sync
        def _(sync):
            E.c("ph").wait(sync, phn)
            dst = E.d["x1p_s"] if layer == "l0" else E.d["out"]
            if layer == "l1":
                rld.inc(sync.dma_start(out=s1d[:], in_=E.d["s1d_b"]), 16)

            def spill_u(u):
                tcs = (u % 16) // 4 * 4 + u % 4
                dbs = u // 16
                vdn.wait(sync, vdn0 + u + 1)
                ospl.inc(sync.dma_start(
                    out=dst[tcs * 128:(tcs + 1) * 128, dbs * 512:(dbs + 1) * 512],
                    in_=osb[u % 4][:]), 16)

            for dblk in range(4):
                for tg in range(4):
                    for tl in range(4):
                        u = dblk * 16 + tg * 4 + tl
                        tc = tg * 4 + tl
                        if u >= 2:
                            mdn.wait(sync, mdn0 + u - 1)
                            vdn.wait(sync, vdn0 + u - 1)
                        for grp in range(2):
                            gld.inc(sync.dma_start(
                                out=gb[u % 2][:, grp * 16 * 128:(grp + 1) * 16 * 128]
                                .rearrange("p (c t) -> p c t", t=128),
                                in_=E.d["g_s"][tg, grp * 2048:(grp + 1) * 2048,
                                               tl * 128:(tl + 1) * 128]
                                .rearrange("(c p) t -> p c t", p=128)), 16)
                        if layer == "l0":
                            rld.inc(sync.dma_start(
                                out=res1[u % 2][:],
                                in_=E.d["x"][tc * 128:(tc + 1) * 128,
                                             dblk * 512:(dblk + 1) * 512]), 16)
                        else:
                            rld.inc(sync.dma_start(
                                out=res1[u % 2][:],
                                in_=E.d["x1_s"][tc * 128:(tc + 1) * 128,
                                                dblk * 512:(dblk + 1) * 512]), 16)
                            rld.inc(sync.dma_start(
                                out=res2[u % 2][:],
                                in_=E.d["y_s"][tc, :, dblk * 512:(dblk + 1) * 512]), 16)
                        if u >= 1:
                            spill_u(u - 1)
            spill_u(63)

        @E.block.scalar
        def _(sq):
            E.c("ph").wait(sq, phn)
            for dblk in range(4):
                for tg in range(4):
                    for tl in range(4):
                        u = dblk * 16 + tg * 4 + tl
                        if u >= 2:
                            mdn.wait(sq, mdn0 + u - 1)
                        for grp in range(2, 4):
                            gld2.inc(sq.dma_start(
                                out=gb[u % 2][:, grp * 16 * 128:(grp + 1) * 16 * 128]
                                .rearrange("p (c t) -> p c t", t=128),
                                in_=E.d["g_s"][tg, grp * 2048:(grp + 1) * 2048,
                                               tl * 128:(tl + 1) * 128]
                                .rearrange("(c p) t -> p c t", p=128)), 16)

        @E.block.tensor
        def _(t):
            E.c("ph").wait(t, phn)
            for dblk in range(4):
                wld.wait(t, wld0 + 16 * 8 * (dblk + 1))
                for tg in range(4):
                    for tl in range(4):
                        u = dblk * 16 + tg * 4 + tl
                        gld.wait(t, gld0 + 32 * (u + 1))
                        gld2.wait(t, gld20 + 32 * (u + 1))
                        if u >= 2:
                            vdn.wait(t, vdn0 + u - 1)
                        for hc in range(NHC):
                            ins = t.matmul(psD[u % 2][:],
                                           gb[u % 2][:, hc * 128:(hc + 1) * 128],
                                           wd[:, hc * 512:(hc + 1) * 512],
                                           start=(hc == 0), stop=(hc == NHC - 1))
                        mdn.inc(ins)

        @E.block.vector
        def _(v):
            E.c("ph").wait(v, phn)
            for dblk in range(4):
                for tg in range(4):
                    for tl in range(4):
                        u = dblk * 16 + tg * 4 + tl
                        mdn.wait(v, mdn0 + u + 1)
                        rld.wait(v, rld0 + 16 * rper * (u + 1) + (16 if layer == "l1" else 0))
                        if u >= 4:
                            ospl.wait(v, ospl0 + 16 * (u - 3))
                        if layer == "l0":
                            vdn.inc(v.tensor_tensor(out=osb[u % 4][:], in0=psD[u % 2][:],
                                                    in1=res1[u % 2][:], op=ALU.add))
                        else:
                            v.tensor_tensor(out=tmp[:], in0=psD[u % 2][:],
                                            in1=s1d[:, dblk * 512:(dblk + 1) * 512],
                                            op=ALU.mult)
                            v.tensor_tensor(out=tmp[:], in0=tmp[:], in1=res1[u % 2][:],
                                            op=ALU.add)
                            vdn.inc(v.tensor_tensor(out=osb[u % 4][:], in0=tmp[:],
                                                    in1=res2[u % 2][:], op=ALU.add))

        E.down_done = (ospl, ospl0 + 16 * 64)
        ospl.n = ospl0 + 16 * 64



def emit_outproj(E):
    nc = E.nc
    phn = E.c("ph").n
    with ExitStack() as phs:
        e = phs.enter_context
        wo = e(nc.sbuf_tensor("op_wo", [128, NFC * D], BF16))
        zf = [e(nc.sbuf_tensor(f"op_zf{i}", [128, NFC * 128], BF16)) for i in range(2)]
        xp = [e(nc.sbuf_tensor(f"op_xp{i}", [128, D], F32)) for i in range(2)]
        x1sb = [e(nc.sbuf_tensor(f"op_x1{i}", [128, D], F32)) for i in range(2)]
        psO = [e(nc.psum_tensor(f"op_psO{i}", [128, 512], F32)) for i in range(2)]

        wld = E.c("op_wld")
        zld = E.c("op_zld")
        mop = E.c("op_mop")
        vop = E.c("op_vop")
        xspl = E.c("op_xspl")
        wld0, zld0, mop0, vop0, xspl0 = wld.n, zld.n, mop.n, vop.n, xspl.n

        @E.block.gpsimd
        def _(g):
            E.c("ph").wait(g, phn)
            for i in range(4):
                wld.inc(g.dma_start(
                    out=wo[:, i * 4 * D:(i + 1) * 4 * D].rearrange("p (c o) -> p c o", o=D),
                    in_=E.d["wout_t"][i * 512:(i + 1) * 512, :]
                    .rearrange("(c p) o -> p c o", p=128)), 16)

        @E.block.sync
        def _(sync):
            E.c("ph").wait(sync, phn)

            def spill_tile(tc):
                vop.wait(sync, vop0 + 4 * (tc + 1))
                xspl.inc(sync.dma_start(
                    out=E.d["x1_s"][tc * 128:(tc + 1) * 128, :],
                    in_=x1sb[tc % 2][:]), 16)

            for tc in range(NTC):
                if tc >= 2:
                    mop.wait(sync, mop0 + (tc - 1) * 4)
                    vop.wait(sync, vop0 + (tc - 1) * 4)
                zld.inc(sync.dma_start(
                    out=zf[tc % 2][:].rearrange("p (c t) -> p c t", t=128),
                    in_=E.d["z_s"][tc].rearrange("(c p) t -> p c t", p=128)), 16)
                zld.inc(sync.dma_start(
                    out=xp[tc % 2][:],
                    in_=E.d["x1p_s"][tc * 128:(tc + 1) * 128, :]), 16)
                if tc >= 1:
                    spill_tile(tc - 1)
            spill_tile(NTC - 1)

        @E.block.tensor
        def _(t):
            E.c("ph").wait(t, phn)
            wld.wait(t, wld0 + 64)
            for tc in range(NTC):
                zld.wait(t, zld0 + 32 * (tc + 1))
                for og in range(4):
                    w = tc * 4 + og
                    if w >= 2:
                        vop.wait(t, vop0 + w - 1)
                    for fc in range(NFC):
                        ins = t.matmul(
                            psO[w % 2][:], zf[tc % 2][:, fc * 128:(fc + 1) * 128],
                            wo[:, fc * D + og * 512: fc * D + (og + 1) * 512],
                            start=(fc == 0), stop=(fc == NFC - 1))
                    mop.inc(ins)

        @E.block.vector
        def _(v):
            E.c("ph").wait(v, phn)
            for tc in range(NTC):
                for og in range(4):
                    w = tc * 4 + og
                    mop.wait(v, mop0 + w + 1)
                    if og == 0 and tc >= 2:
                        xspl.wait(v, xspl0 + 16 * (tc - 1))
                    vop.inc(v.tensor_tensor(
                        out=x1sb[tc % 2][:, og * 512:(og + 1) * 512],
                        in0=psO[w % 2][:], in1=xp[tc % 2][:, og * 512:(og + 1) * 512],
                        op=ALU.add))

        E.op_done = (xspl, xspl0 + 16 * NTC)
        xspl.n = xspl0 + 16 * NTC



def emit_l1_gates_scan(E):
    nc = E.nc
    phn = E.c("ph").n
    with ExitStack() as phs:
        e = phs.enter_context
        at = [e(nc.sbuf_tensor(f"g1_a{i}", [128, TOK], F32)) for i in range(2)]
        nxt = [e(nc.sbuf_tensor(f"g1_nx{i}", [128, TOK], F32)) for i in range(2)]
        yl = e(nc.sbuf_tensor("g1_yl", [128, TOK], F32))
        gx = [e(nc.sbuf_tensor(f"g1_gx{i}", [128, 512], F32)) for i in range(2)]
        ga = [e(nc.sbuf_tensor(f"g1_ga{i}", [128, 512], F32)) for i in range(2)]
        la = [e(nc.sbuf_tensor(f"g1_la{i}", [128, 512], F32)) for i in range(2)]
        e2 = [e(nc.sbuf_tensor(f"g1_e2{i}", [128, 512], F32)) for i in range(2)]
        om = [e(nc.sbuf_tensor(f"g1_om{i}", [128, 512], F32)) for i in range(2)]
        ml = [e(nc.sbuf_tensor(f"g1_ml{i}", [128, 512], F32)) for i in range(2)]
        gxx = e(nc.sbuf_tensor("g1_gxx", [128, 512], F32))
        S1sb = e(nc.sbuf_tensor("g1_S1", [128, H], F32))
        psX = [e(nc.psum_tensor(f"g1_psX{i}", [128, 512], F32)) for i in range(2)]
        psA_ = [e(nc.psum_tensor(f"g1_psA{i}", [128, 512], F32)) for i in range(2)]

        mgx = E.c("g1_mgx")
        sg1 = E.c("g1_sg1")
        v1 = E.c("g1_v1")
        s2 = E.c("g1_s2")
        v2 = E.c("g1_v2")
        s3 = E.c("g1_s3")
        vml = E.c("g1_vml")
        v3 = E.c("g1_v3")
        vsc = E.c("g1_vsc")
        ss1 = E.c("g1_ss1")
        aspl = E.c("g1_aspl")
        (mgx0, sg10, v10, s20, v20, s30, vml0, v30, vsc0, ss10, aspl0) = (
            mgx.n, sg1.n, v1.n, s2.n, v2.n, s3.n, vml.n, v3.n, vsc.n, ss1.n, aspl.n)

        @E.block.tensor
        def _(t):
            E.c("ph").wait(t, phn)
            for hc in range(H):
                for tg in range(4):
                    z = hc * 4 + tg
                    if z >= 2:
                        sg1.wait(t, sg10 + z - 1)
                    t.matmul(psX[z % 2][:], E.t["igw"][:, hc * 128:(hc + 1) * 128],
                             E.t["xnf"][:, hc * TOK + tg * 512: hc * TOK + (tg + 1) * 512],
                             start=True, stop=True)
                    ins = t.matmul(psA_[z % 2][:], E.t["agw"][:, hc * 128:(hc + 1) * 128],
                                   E.t["xnf"][:, hc * TOK + tg * 512: hc * TOK + (tg + 1) * 512],
                                   start=True, stop=True)
                    mgx.inc(ins, 2)

        @E.block.scalar
        def _(s):
            E.c("ph").wait(s, phn)
            for hc in range(H):
                for tg in range(4):
                    z = hc * 4 + tg
                    mgx.wait(s, mgx0 + 2 * (z + 1))
                    if z >= 2:
                        v1.wait(s, v10 + z - 1)
                        v3.wait(s, v30 + z - 1)
                    s.activation(out=gx[z % 2][:], in_=psX[z % 2][:], func=AF.Sigmoid,
                                 bias=E.t["igb"][:, hc:hc + 1])
                    sg1.inc(s.activation(out=ga[z % 2][:], in_=psA_[z % 2][:],
                                         func=AF.Sigmoid, bias=E.t["agb"][:, hc:hc + 1]))
                    v1.wait(s, v10 + z + 1)
                    if hc >= 2 and tg == 0:
                        vsc.wait(s, vsc0 + hc - 1)
                        aspl.wait(s, aspl0 + 32 * (hc - 1))
                    s.activation(out=at[hc % 2][:, tg * 512:(tg + 1) * 512],
                                 in_=la[z % 2][:], func=AF.Exp)
                    s2.inc(s.activation(out=e2[z % 2][:], in_=la[z % 2][:],
                                        func=AF.Exp, scale=2.0))
                    v2.wait(s, v20 + z + 1)
                    s3.inc(s.activation(out=ml[z % 2][:], in_=om[z % 2][:], func=AF.Sqrt))
                if hc >= 1:
                    vsc.wait(s, vsc0 + hc)
                    ss1.inc(s.copy(out=S1sb[:, hc - 1:hc], in_=yl[:, TOK - 1:TOK]))

        @E.block.vector
        def _(v):
            E.c("ph").wait(v, phn)
            for hc in range(H):
                for tg in range(4):
                    z = hc * 4 + tg
                    sg1.wait(v, sg10 + z + 1)
                    if z >= 2:
                        s2.wait(v, s20 + z - 1)
                    v1.inc(v.tensor_scalar_mul(out=la[z % 2][:], in0=ga[z % 2][:],
                                               scalar1=E.t["m8spa"][:, hc:hc + 1]))
                    s2.wait(v, s20 + z + 1)
                    if z >= 2:
                        s3.wait(v, s30 + z - 1)
                    v2.inc(v.tensor_scalar(out=om[z % 2][:], in0=e2[z % 2][:],
                                           scalar1=-1.0, scalar2=1.0,
                                           op0=ALU.mult, op1=ALU.add))
                    s3.wait(v, s30 + z + 1)
                    if tg == 0:
                        vml.inc(v.tensor_scalar(out=ml[z % 2][:, 0:1], in0=ml[z % 2][:, 0:1],
                                                scalar1=E.t["omrcol"][:],
                                                scalar2=E.t["rcol"][:],
                                                op0=ALU.mult, op1=ALU.add))
                        vml.wait(v)
                    v.tensor_tensor(out=gxx[:],
                                    in0=E.t["xnf"][:, hc * TOK + tg * 512: hc * TOK + (tg + 1) * 512],
                                    in1=gx[z % 2][:], op=ALU.mult)
                    if hc >= 2 and tg == 0:
                        aspl.wait(v, aspl0 + 32 * (hc - 1))
                    v3.inc(v.tensor_tensor(out=nxt[hc % 2][:, tg * 512:(tg + 1) * 512],
                                           in0=gxx[:], in1=ml[z % 2][:], op=ALU.mult))
                s2.wait(v, s20 + (hc * 4 + 4))
                if hc >= 1:
                    ss1.wait(v, ss10 + hc)
                vsc.inc(v.tensor_tensor_scan(out=yl[:], data0=at[hc % 2][:],
                                             data1=nxt[hc % 2][:], initial=0.0,
                                             op0=ALU.mult, op1=ALU.add))

        @E.block.scalar
        def _(s):
            vsc.wait(s, vsc0 + H)
            ss1.inc(s.copy(out=S1sb[:, H - 1:H], in_=yl[:, TOK - 1:TOK]))

        @E.block.sync
        def _(sync):
            for hc in range(H):
                s2.wait(sync, s20 + hc * 4 + 4)
                v3.wait(sync, v30 + (hc + 1) * 4)
                aspl.inc(sync.dma_start(out=E.d["a_s"][hc], in_=at[hc % 2][:]), 16)
                aspl.inc(sync.dma_start(out=E.d["nx_s"][hc], in_=nxt[hc % 2][:]), 16)
            ss1.wait(sync, ss10 + H)
            aspl.inc(sync.dma_start(out=E.d["S1_out"], in_=S1sb[:]), 16)
        E.g1_done = (aspl, aspl.n)



def emit_s1route(E, src_ap, cc_wait=None):
    nc = E.nc
    st = E.stack.enter_context
    E.t["S1i"] = st(nc.sbuf_tensor("S1i", [128, H], F32))
    phn = E.c("ph").n
    ld = E.c("ld_s1route")
    srt = E.c("s1rt")

    @E.block.sync
    def _(sync):
        E.c("ph").wait(sync, phn)
        if cc_wait is not None:
            cc_wait[0].wait(sync, cc_wait[1])
        ld.inc(sync.dma_start(out=E.t["S1i"][:], in_=src_ap), 16)

    @E.block.vector
    def _(v):
        E.c("ph").wait(v, phn)
        ld.wait(v)
        srt.inc(v.tensor_scalar_mul(out=E.t["S1i"][:], in0=E.t["S1i"][:],
                                    scalar1=E.t["selp"][:]))
    E.s1route_done = (srt, srt.n)



def emit_scan2_y(E):
    nc = E.nc
    phn = E.c("ph").n
    with ExitStack() as phs:
        e = phs.enter_context
        a2 = [e(nc.sbuf_tensor(f"s2_a{i}", [128, TOK], F32)) for i in range(2)]
        nx2 = [e(nc.sbuf_tensor(f"s2_nx{i}", [128, TOK], F32)) for i in range(2)]
        yf = [e(nc.sbuf_tensor(f"s2_y{i}", [128, TOK], F32)) for i in range(2)]
        yev = [e(nc.sbuf_tensor(f"s2_ye{i}", [128, 128], F32)) for i in range(4)]
        psY = [e(nc.psum_tensor(f"s2_psY{i}", [128, 512], F32)) for i in range(2)]

        ld2 = E.c("s2_ld")
        vs2 = E.c("s2_vs2")
        ytp = E.c("s2_ytp")
        yec = E.c("s2_yec")
        yspl = E.c("s2_yspl")
        ld20, vs20, ytp0, yec0, yspl0 = ld2.n, vs2.n, ytp.n, yec.n, yspl.n

        @E.block.sync
        def _(sync):
            E.c("ph").wait(sync, phn)

            def spill_hc(hc):
                for tc in range(NTC):
                    k = hc * NTC + tc
                    yec.wait(sync, yec0 + k + 1)
                    yspl.inc(sync.dma_start(
                        out=E.d["y_s"][tc, :, hc * 128:(hc + 1) * 128],
                        in_=yev[k % 4][:]), 16)

            for hc in range(H):
                if hc >= 2:
                    vs2.wait(sync, vs20 + hc - 1)
                ld2.inc(sync.dma_start(out=a2[hc % 2][:], in_=E.d["a_s"][hc]), 16)
                ld2.inc(sync.dma_start(out=nx2[hc % 2][:], in_=E.d["nx_s"][hc]), 16)
                if hc >= 1:
                    spill_hc(hc - 1)
            spill_hc(H - 1)

        @E.block.vector
        def _(v):
            E.c("ph").wait(v, phn)
            ctr, n = E.s1route_done
            ctr.wait(v, n)
            for hc in range(H):
                ld2.wait(v, ld20 + 32 * (hc + 1))
                if hc >= 2:
                    ytp.wait(v, ytp0 + (hc - 1) * NTC)
                vs2.inc(v.tensor_tensor_scan(out=yf[hc % 2][:], data0=a2[hc % 2][:],
                                             data1=nx2[hc % 2][:],
                                             initial=E.t["S1i"][:, hc:hc + 1],
                                             op0=ALU.mult, op1=ALU.add))
                for tc in range(NTC):
                    k = hc * NTC + tc
                    ytp.wait(v, ytp0 + k + 1)
                    if k >= 4:
                        yspl.wait(v, yspl0 + 16 * (k - 3))
                    yec.inc(v.tensor_copy(out=yev[k % 4][:], in_=psY[k % 2][:, 0:128]))

        @E.block.tensor
        def _(t):
            E.c("ph").wait(t, phn)
            for hc in range(H):
                vs2.wait(t, vs20 + hc + 1)
                for tc in range(NTC):
                    k = hc * NTC + tc
                    if k >= 2:
                        yec.wait(t, yec0 + k - 1)
                    ytp.inc(t.transpose(psY[k % 2][:, 0:128],
                                        yf[hc % 2][:, tc * 128:(tc + 1) * 128],
                                        E.t["identf"][:]))

        E.s2_done = (yspl, yspl0 + 16 * H * NTC)
        yspl.n = yspl0 + 16 * H * NTC



def emit_xnf_load(E, src_ap):
    nc = E.nc
    st = E.stack.enter_context
    if "xnf" not in E.t:
        E.t["xnf"] = st(nc.sbuf_tensor("xnf", [128, NFC * TOK], BF16))
    ld = E.c("ld_xnf")

    @E.block.sync
    def _(sync):
        ld.inc(sync.dma_start(out=E.t["xnf"][:], in_=src_ap), 16)
    E.xnf_done = (ld, ld.n)


def emit_xnf_spill(E, dst_ap):
    sp = E.c("sp_xnf")
    phn = E.c("ph").n

    @E.block.sync
    def _(sync):
        E.c("ph").wait(sync, phn)
        sp.inc(sync.dma_start(out=dst_ap, in_=E.t["xnf"][:]), 16)
    E.xnfspill_done = (sp, sp.n)



def _setup_f32_ident(E):
    nc = E.nc
    E.t["identf"] = E.stack.enter_context(nc.sbuf_tensor("identf", [128, 128], F32))
    gpf = E.c("gp_identf")

    @E.block.gpsimd
    def _(g):
        g.memset(E.t["identf"][:], 0)
        gpf.inc(g.affine_select(out=E.t["identf"][:], in_=E.t["identf"][:],
                                compare_op=ALU.not_equal, fill=1.0, base=0,
                                pattern=[[-1, 128]], channel_multiplier=1))
    return gpf


def build_B():
    nc = bass.Bass("TRN2", target_bir_lowering=False, debug=False, detect_race_conditions=False)
    d = {}
    for name, shape, dt in [
            ("x", [TOK, D], F32), ("slope_c", [128, H], F32), ("selp", [128, 1], F32),
            ("wgate_t", [D, D], F32), ("wout_t", [D, D], F32), ("normw_b", [128, D], F32),
            ("w0g_t", [D, FFN], F32), ("w0u_t", [D, FFN], F32), ("w0d_t", [FFN, D], F32),
            ("igw", [H, 128, 128], F32), ("agw", [H, 128, 128], F32),
            ("igb_c", [128, H], F32), ("agb_c", [128, H], F32), ("apar_c", [128, H], F32),
            ("rcol", [128, 1], F32), ("omrcol", [128, 1], F32),
            ("attn_s", [TOK, D], F32), ("q_s", [H, NTC, 128, 128], BF16),
            ("S_in", [D, DH], F32)]:
        d[name] = nc.dram_tensor(name, shape, dt, kind="ExternalInput").ap()
    for name, shape, dt in [
            ("x1_s", [TOK, D], F32), ("S1_out", [128, H], F32),
            ("a_s", [H, 128, TOK], F32), ("nx_s", [H, 128, TOK], F32),
            ("xn1_s", [128, NFC * TOK], BF16)]:
        d[name] = nc.dram_tensor(name, shape, dt, kind="ExternalOutput").ap()
    for name, shape, dt in [
            ("x1p_s", [TOK, D], F32), ("g_s", [4, FFN, 512], BF16),
            ("z_s", [NTC, D, 128], BF16)]:
        d[name] = nc.dram_tensor(name, shape, dt, kind="Internal").ap()
    with ExitStack() as stack:
        block = stack.enter_context(nc.Block())
        E = Emit(nc, stack, block)
        E.d = d
        ph = E.c("ph")
        emit_setup(E, "B")
        emit_setup2(E, "B")
        scalar_fence(E, [(E.setup_done, E.setup_done.n), E.setup2_done,
                         E.setup2g_done], ph)
        emit_ln(E, d["x"], "l0")
        scalar_fence(E, [E.ln_done], ph)
        emit_sroute(E, d["S_in"])
        emit_corr_rms_gate_z(E)
        scalar_fence(E, [E.crz_done], ph)
        emit_ffn_gu(E, "l0")
        scalar_fence(E, [E.gu_done], ph)
        emit_ffn_down(E, "l0")
        scalar_fence(E, [E.down_done], ph)
        emit_outproj(E)
        scalar_fence(E, [E.op_done], ph)
        emit_ln(E, d["x1_s"], "l1")
        scalar_fence(E, [E.ln_done], ph)
        emit_xnf_spill(E, d["xn1_s"])
        emit_l1_gates_scan(E)

        @E.block.gpsimd
        def _(g):
            ctr, n = E.g1_done
            ctr.wait(g, n)
            ctr, n = E.xnfspill_done
            ctr.wait(g, n)
    return nc


def build_C():
    nc = bass.Bass("TRN2", target_bir_lowering=False, debug=False, detect_race_conditions=False)
    d = {}
    for name, shape, dt in [
            ("slope_c", [128, H], F32), ("selp", [128, 1], F32),
            ("x1_s", [TOK, D], F32), ("S1_in", [128, H], F32),
            ("a_s", [H, 128, TOK], F32), ("nx_s", [H, 128, TOK], F32),
            ("xn1_s", [128, NFC * TOK], BF16),
            ("w1g", [D, FFN], I8), ("w1u", [D, FFN], I8), ("w1d", [FFN, D], I8),
            ("s1g_c", [128, 64], F32), ("s1u_c", [128, 64], F32), ("s1d_b", [128, D], F32)]:
        d[name] = nc.dram_tensor(name, shape, dt, kind="ExternalInput").ap()
    d["out"] = nc.dram_tensor("out", [TOK, D], F32, kind="ExternalOutput").ap()
    for name, shape, dt in [
            ("g_s", [4, FFN, 512], BF16), ("y_s", [NTC, 128, D], F32)]:
        d[name] = nc.dram_tensor(name, shape, dt, kind="Internal").ap()
    with ExitStack() as stack:
        block = stack.enter_context(nc.Block())
        E = Emit(nc, stack, block)
        E.d = d
        ph = E.c("ph")
        emit_setup(E, "C")
        emit_setup2(E, "C")
        gpf = _setup_f32_ident(E)
        emit_xnf_load(E, d["xn1_s"])
        scalar_fence(E, [(E.setup_done, E.setup_done.n), E.setup2_done,
                         (gpf, gpf.n), E.xnf_done], ph)
        emit_s1route(E, d["S1_in"])
        emit_scan2_y(E)
        scalar_fence(E, [E.s2_done], ph)
        emit_ffn_gu(E, "l1")
        scalar_fence(E, [E.gu_done], ph)
        emit_ffn_down(E, "l1")

        @E.block.gpsimd
        def _(g):
            ctr, n = E.down_done
            ctr.wait(g, n)
    return nc



def _prep_common(inputs):
    c = {}
    c["wqkv_t"] = np.ascontiguousarray(np.asarray(inputs["L0_attn_w_qkv"]).T)
    c["slope_c"] = np.ascontiguousarray(
        np.broadcast_to(np.asarray(inputs["L0_attn_slope_rate"])[None, :], (128, H)))
    c["wgate_t"] = np.ascontiguousarray(np.asarray(inputs["L0_attn_w_gate"]).T)
    c["wout_t"] = np.ascontiguousarray(np.asarray(inputs["L0_attn_w_out"]).T)
    c["normw_b"] = np.ascontiguousarray(
        np.broadcast_to(np.asarray(inputs["L0_attn_norm_w"])[None, :], (128, D)))
    c["w0g_t"] = np.ascontiguousarray(np.asarray(inputs["L0_ffn_w_gate"]).T)
    c["w0u_t"] = np.ascontiguousarray(np.asarray(inputs["L0_ffn_w_up"]).T)
    c["w0d_t"] = np.ascontiguousarray(np.asarray(inputs["L0_ffn_w_down"]).T)
    c["igw"] = np.ascontiguousarray(np.asarray(inputs["L1_attn_ig_w"], np.float32))
    c["agw"] = np.ascontiguousarray(np.asarray(inputs["L1_attn_ag_w"], np.float32))
    c["igb_c"] = np.ascontiguousarray(np.asarray(inputs["L1_attn_ig_b"], np.float32).T)
    c["agb_c"] = np.ascontiguousarray(np.asarray(inputs["L1_attn_ag_b"], np.float32).T)
    c["apar_c"] = np.ascontiguousarray(
        np.asarray(inputs["L1_attn_a_param"], np.float32).reshape(H, 128).T)
    c["w1g"] = np.ascontiguousarray(np.asarray(inputs["L1_ffn_w_gate"]))
    c["w1u"] = np.ascontiguousarray(np.asarray(inputs["L1_ffn_w_up"]))
    c["w1d"] = np.ascontiguousarray(np.asarray(inputs["L1_ffn_w_down"]))
    c["s1g_c"] = np.ascontiguousarray(
        np.asarray(inputs["L1_ffn_s_gate"], np.float32).reshape(64, 128).T)
    c["s1u_c"] = np.ascontiguousarray(
        np.asarray(inputs["L1_ffn_s_up"], np.float32).reshape(64, 128).T)
    c["s1d_b"] = np.ascontiguousarray(
        np.broadcast_to(np.asarray(inputs["L1_ffn_s_down"], np.float32)[None, :], (128, D)))
    return c


def _core_inputs_B(inputs, common, core, attn_s, q_s, S_in):
    b, half = core // 2, core % 2
    x = np.ascontiguousarray(
        np.asarray(inputs["L0_attn_hidden_states"])[b, half * TOK:(half + 1) * TOK])
    selp = np.full((128, 1), 1.0 if half == 1 else 0.0, np.float32)
    rcol = np.full((128, 1), 1.0 if half == 0 else 0.0, np.float32)
    omrcol = np.full((128, 1), 0.0 if half == 0 else 1.0, np.float32)
    m = dict(x=x, selp=selp, rcol=rcol, omrcol=omrcol,
             attn_s=attn_s, q_s=q_s, S_in=S_in)
    for k in ("slope_c", "wgate_t", "wout_t", "normw_b", "w0g_t", "w0u_t",
              "w0d_t", "igw", "agw", "igb_c", "agb_c", "apar_c"):
        m[k] = common[k]
    return m


def _core_inputs_C(inputs, common, core, x1_s, S1_in, a_s, nx_s, xn1_s):
    half = core % 2
    selp = np.full((128, 1), 1.0 if half == 1 else 0.0, np.float32)
    m = dict(selp=selp, x1_s=x1_s, S1_in=S1_in, a_s=a_s, nx_s=nx_s, xn1_s=xn1_s)
    for k in ("slope_c", "w1g", "w1u", "w1d", "s1g_c", "s1u_c", "s1d_b"):
        m[k] = common[k]
    return m


B_, SEQ = 4, 4096


FUSED = True


def kernel(**inputs):
    if FUSED:
        return kernel_fused(inputs)
    return kernel_unfused(inputs)


def kernel_unfused(inputs):
    common = _prep_common(inputs)
    ncA = build_A()
    in_A = [_core_inputs_A(inputs, c) for c in range(8)]
    resA = run_bass_kernel_spmd(ncA, in_A, core_ids=list(range(8))).results
    zS = np.zeros((D, DH), np.float32)
    S_ins = [resA[c - 1]["S_out"] if c % 2 == 1 else zS for c in range(8)]
    ncB = build_B()
    in_B = [_core_inputs_B(inputs, common, c, resA[c]["attn_s"], resA[c]["q_s"],
                           S_ins[c]) for c in range(8)]
    resB = run_bass_kernel_spmd(ncB, in_B, core_ids=list(range(8))).results
    zS1 = np.zeros((128, H), np.float32)
    S1_ins = [resB[c - 1]["S1_out"] if c % 2 == 1 else zS1 for c in range(8)]
    ncC = build_C()
    in_C = [_core_inputs_C(inputs, common, c, resB[c]["x1_s"], S1_ins[c],
                           resB[c]["a_s"], resB[c]["nx_s"], resB[c]["xn1_s"])
            for c in range(8)]
    resC = run_bass_kernel_spmd(ncC, in_C, core_ids=list(range(8))).results
    out = np.zeros((B_, SEQ, D), np.float32)
    for c in range(8):
        b, half = c // 2, c % 2
        out[b, half * TOK:(half + 1) * TOK] = resC[c]["out"]
    return out



PAIRS = [[0, 1], [2, 3], [4, 5], [6, 7]]


def build_FULL():
    nc = bass.Bass("TRN2", target_bir_lowering=False, debug=False,
                   detect_race_conditions=False, num_devices=8)
    d = {}
    for name, shape, dt in [
            ("x", [TOK, D], F32), ("wqkv_t", [D, 3 * D], F32),
            ("slope_c", [128, H], F32), ("selp", [128, 1], F32),
            ("wgate_t", [D, D], F32), ("wout_t", [D, D], F32), ("normw_b", [128, D], F32),
            ("w0g_t", [D, FFN], F32), ("w0u_t", [D, FFN], F32), ("w0d_t", [FFN, D], F32),
            ("igw", [H, 128, 128], F32), ("agw", [H, 128, 128], F32),
            ("igb_c", [128, H], F32), ("agb_c", [128, H], F32), ("apar_c", [128, H], F32),
            ("rcol", [128, 1], F32), ("omrcol", [128, 1], F32),
            ("w1g", [D, FFN], I8), ("w1u", [D, FFN], I8), ("w1d", [FFN, D], I8),
            ("s1g_c", [128, 64], F32), ("s1u_c", [128, 64], F32), ("s1d_b", [128, D], F32)]:
        d[name] = nc.dram_tensor(name, shape, dt, kind="ExternalInput").ap()
    d["out"] = nc.dram_tensor("out", [TOK, D], F32, kind="ExternalOutput").ap()
    for name, shape, dt in [
            ("attn_s", [TOK, D], F32), ("q_s", [H, NTC, 128, 128], BF16),
            ("x1p_s", [TOK, D], F32), ("x1_s", [TOK, D], F32),
            ("g_s", [4, FFN, 512], BF16), ("z_s", [NTC, D, 128], BF16),
            ("y_s", [NTC, 128, D], F32),
            ("a_s", [H, 128, TOK], F32), ("nx_s", [H, 128, TOK], F32),
            ("S_cc_in", [D, DH], F32), ("S_cc_out", [2 * D, DH], F32),
            ("S1_cc_in", [128, H], F32), ("S1_cc_out", [256, H], F32)]:
        d[name] = nc.dram_tensor(name, shape, dt, kind="Internal").ap()
    d["S_out"] = d["S_cc_in"]
    d["S1_out"] = d["S1_cc_in"]
    with ExitStack() as stack:
        block = stack.enter_context(nc.Block())
        E = Emit(nc, stack, block)
        E.d = d
        ph = E.c("ph")
        E.t["xnf"] = stack.enter_context(nc.sbuf_tensor("xnf", [128, NFC * TOK], BF16))
        emit_setup(E, "FULL")
        emit_setup2(E, "FULL")
        gpf = _setup_f32_ident(E)
        scalar_fence(E, [(E.setup_done, E.setup_done.n), E.setup2_done,
                         E.setup2g_done, (gpf, gpf.n)], ph)
        with ExitStack() as ascope:
            emit_attn_consts(E, ascope.enter_context)
            scalar_fence(E, [E.aconst_done], ph)
            emit_ln(E, d["x"], "l0")
            scalar_fence(E, [E.ln_done], ph)
            emit_attn_local(E)
            cc1 = E.c("cc1")

            @E.block.gpsimd
            def _(g):
                ctr, n = E.attn_done
                ctr.wait(g, n)
                cc1.inc(g.collective_compute(
                    "AllGather", ALU.bypass, replica_groups=PAIRS,
                    ins=[d["S_cc_in"]], outs=[d["S_cc_out"]]))
            scalar_fence(E, [E.attn_done], ph)
        with ExitStack() as cscope:
            emit_sroute(E, d["S_cc_out"][0:D, :], cc_wait=(cc1, cc1.n),
                        alloc=cscope.enter_context)
            scalar_fence(E, [E.sroute_done], ph)
            emit_corr_rms_gate_z(E)
            scalar_fence(E, [E.crz_done], ph)
        emit_ffn_gu(E, "l0")
        scalar_fence(E, [E.gu_done], ph)
        emit_ffn_down(E, "l0")
        scalar_fence(E, [E.down_done], ph)
        emit_outproj(E)
        scalar_fence(E, [E.op_done], ph)
        emit_ln(E, d["x1_s"], "l1")
        scalar_fence(E, [E.ln_done], ph)
        emit_l1_gates_scan(E)
        cc2 = E.c("cc2")

        @E.block.gpsimd
        def _(g):
            ctr, n = E.g1_done
            ctr.wait(g, n)
            cc2.inc(g.collective_compute(
                "AllGather", ALU.bypass, replica_groups=PAIRS,
                ins=[d["S1_cc_in"]], outs=[d["S1_cc_out"]]))
        scalar_fence(E, [E.g1_done], ph)
        emit_s1route(E, d["S1_cc_out"][0:128, :], cc_wait=(cc2, cc2.n))
        emit_scan2_y(E)
        scalar_fence(E, [E.s2_done], ph)
        emit_ffn_gu(E, "l1")
        scalar_fence(E, [E.gu_done], ph)
        emit_ffn_down(E, "l1")

        @E.block.gpsimd
        def _(g):
            ctr, n = E.down_done
            ctr.wait(g, n)
    return nc


def _core_inputs_FULL(inputs, common, core):
    b, half = core // 2, core % 2
    x = np.ascontiguousarray(
        np.asarray(inputs["L0_attn_hidden_states"])[b, half * TOK:(half + 1) * TOK])
    selp = np.full((128, 1), 1.0 if half == 1 else 0.0, np.float32)
    rcol = np.full((128, 1), 1.0 if half == 0 else 0.0, np.float32)
    omrcol = np.full((128, 1), 0.0 if half == 0 else 1.0, np.float32)
    m = dict(x=x, selp=selp, rcol=rcol, omrcol=omrcol)
    for k in ("wqkv_t", "slope_c", "wgate_t", "wout_t", "normw_b", "w0g_t", "w0u_t",
              "w0d_t", "igw", "agw", "igb_c", "agb_c", "apar_c",
              "w1g", "w1u", "w1d", "s1g_c", "s1u_c", "s1d_b"):
        m[k] = common[k]
    return m


def kernel_fused(inputs):
    common = _prep_common(inputs)
    ncF = build_FULL()
    in_F = [_core_inputs_FULL(inputs, common, c) for c in range(8)]
    res = run_bass_kernel_spmd(ncF, in_F, core_ids=list(range(8))).results
    out = np.zeros((B_, SEQ, D), np.float32)
    for c in range(8):
        b, half = c // 2, c % 2
        out[b, half * TOK:(half + 1) * TOK] = res[c]["out"]
    return out
